# version 12
# baseline (speedup 1.0000x reference)
import numpy as np
import concourse.bass as bass
import concourse.mybir as mybir
from concourse.bass_utils import run_bass_kernel_spmd

F32 = mybir.dt.float32
BF16 = mybir.dt.bfloat16
ALU = mybir.AluOpType
ACT = mybir.ActivationFunctionType
AX = mybir.AxisListType

B, T, D, I = 2, 16384, 1024, 3072
NCORE = 8
TOK = (B * T) // NCORE
TG = 512
NTG = TOK // TG
DC = D // 128
OC = I // 128
CH = 1024
NCH = TOK // CH


def set_tok(tok):
    global TOK, NTG, NCH, T
    TOK = tok
    NTG = TOK // TG
    NCH = TOK // CH
    T = (NCORE * TOK) // B
MAGIC = 12582912.0
QMAX = 32767.0
WQMAX = 7.0
LN_EPS = 1e-5
ATT_EPS = 1e-6
STAGE = 3
DBG = False

f32 = np.float32


class P:
    ENGS = ("sync", "vector", "scalar", "tensor", "gpsimd")

    def __init__(self, nc):
        self.nc = nc
        self.q = {e: [] for e in self.ENGS}
        self.sems = {}

    def sem(self, name):
        if name not in self.sems:
            self.sems[name] = [self.nc.alloc_semaphore(name), 0]
        return self.sems[name]

    def emit(self, eng, fn, waits=(), inc=None, inc_by=1):
        if inc is not None:
            self.sem(inc)[1] += inc_by
        ws = tuple((sn, v) for sn, v in waits if v > 0)
        for sn, _ in ws:
            self.sem(sn)
        self.q[eng].append((ws, fn, inc, inc_by))
        return self.sems[inc][1] if inc is not None else None

    def dma(self, eng, out, in_, waits=(), inc=None, **kw):
        if inc is None:
            inc = "d_" + eng
        return self.emit(eng, lambda e, o=out, i=in_, kw=kw: e.dma_start(out=o, in_=i, **kw),
                         waits=waits, inc=inc, inc_by=16)

    def n(self, name):
        return self.sem(name)[1]

    def run(self, block):
        for eng in self.ENGS:
            ops = self.q[eng]

            def make(eng_ops, engname):
                def body(proxy):
                    for waits, fn, inc, inc_by in eng_ops:
                        for (sn, val) in waits:
                            proxy.wait_ge(self.sems[sn][0], val)
                        ins = fn(proxy)
                        if inc is not None:
                            assert ins is not None, f"{engname}: inc {inc} got None"
                            ins.then_inc(self.sems[inc][0], inc_by)
                return body
            getattr(block, eng)(make(ops, eng))


def build():
    nc = bass.Bass("TRN2", target_bir_lowering=False, detect_race_conditions=False)

    x_in = nc.dram_tensor("x_in", [D, TOK], F32, kind="ExternalInput").ap()
    wqT_h = nc.dram_tensor("wqT", [D, D], F32, kind="ExternalInput").ap()
    wkT_h = nc.dram_tensor("wkT", [D, D], F32, kind="ExternalInput").ap()
    wvT_h = nc.dram_tensor("wvT", [D, D], F32, kind="ExternalInput").ap()
    woT_h = nc.dram_tensor("woT", [D, D], F32, kind="ExternalInput").ap()
    bq_h = nc.dram_tensor("bq", [128, DC], F32, kind="ExternalInput").ap()
    bk_h = nc.dram_tensor("bk", [128, DC], F32, kind="ExternalInput").ap()
    bv_h = nc.dram_tensor("bvr", [128, D], F32, kind="ExternalInput").ap()
    bo_h = nc.dram_tensor("bo", [128, DC], F32, kind="ExternalInput").ap()
    wg_h = nc.dram_tensor("wg", [I, D], F32, kind="ExternalInput").ap()
    wu_h = nc.dram_tensor("wu", [I, D], F32, kind="ExternalInput").ap()
    wd_h = nc.dram_tensor("wd", [D, I], F32, kind="ExternalInput").ap()
    agm_h = nc.dram_tensor("agmask", [128, 4], F32, kind="ExternalInput").ap()
    out_h = nc.dram_tensor("out", [D, TOK], F32, kind="ExternalOutput").ap()

    KD = dict(kind="ExternalOutput") if DBG else {}
    ln_s = nc.dram_tensor("ln_s", [D, TOK], F32, **KD).ap()
    xq_s = nc.dram_tensor("xq_s", [D, TOK], BF16, **KD).ap()
    gate_s = nc.dram_tensor("gate_s", [I, TOK], F32, **KD).ap()
    up_s = nc.dram_tensor("up_s", [I, TOK], F32, **KD).ap()
    o_sc = nc.dram_tensor("o_sc", [I, TOK], F32, **KD).ap()
    x1_s = nc.dram_tensor("x1_s", [D, TOK], F32, **KD).ap()
    x2_s = nc.dram_tensor("x2_s", [D, TOK], F32, **KD).ap()
    q_s = nc.dram_tensor("q_s", [D, TOK], BF16, **KD).ap()
    num_s = nc.dram_tensor("num_s", [D, TOK], F32, **KD).ap()
    den_s = nc.dram_tensor("den_s", [1, TOK], F32, **KD).ap()
    wgT_s = nc.dram_tensor("wgT_s", [128, DC * I], BF16, **KD).ap()
    wuT_s = nc.dram_tensor("wuT_s", [128, DC * I], BF16, **KD).ap()
    wdT_s = nc.dram_tensor("wdT_s", [128, OC * D], BF16, **KD).ap()
    cc_in = [nc.dram_tensor(f"cci{i}", [4, 1], F32) for i in range(10)]
    cc_out = [nc.dram_tensor(f"cco{i}", [4, 1], F32) for i in range(10)]
    AGW = DC * D + 8
    ag_in = nc.dram_tensor("ag_in", [128, AGW], F32)
    ag_out = nc.dram_tensor("ag_out", [4 * 128, AGW], F32)

    A = nc.alloc_sbuf_tensor
    wA = A("wA", [128, 24576], BF16)
    wB = A("wB", [128, 24576], BF16)
    sIO = A("sIO", [128, 8192], F32)
    sT = A("sT", [128, 4096], F32)
    wnat = A("wnat", [128, 2048], F32)
    wnatq = A("wnatq", [128, 2048], BF16)
    ktm = A("ktm", [128, 4096], BF16, is_transpose_dest=True)
    sBF = A("sBF", [128, 12288], BF16)
    masks = A("masks", [128, 2048], BF16)
    xbts = A("xbts", [128, 2 * TG], BF16)
    x2ts = A("x2ts", [128, 2 * TG], BF16)
    ident = A("ident", [128, 128], F32)
    bcol = A("bcol", [128, 2 * DC], F32)
    bkcol = A("bkcol", [128, DC], F32)
    bvrow = A("bvrow", [128, D], BF16)
    ones_f = A("ones_f", [1, 128], F32)
    ones_cb = A("ones_cb", [128, 1], BF16)
    stat = A("stat", [1, 1024], F32)
    mx = A("mx", [128, 576], F32)
    smax = A("smax", [1, 128], F32)
    scol = A("scol", [128, 16], F32)
    agm = A("agm", [128, 4], F32)
    ksb = A("ksb", [128, 16], BF16)

    pb = [nc.alloc_psum_tensor(f"pb{i}", [128, TG], F32) for i in range(4)]
    pk = nc.alloc_psum_tensor("pk", [128, 1024], F32)
    ps1 = nc.alloc_psum_tensor("ps1", [1, TG], F32)
    ps2 = nc.alloc_psum_tensor("ps2", [1, TG], F32)

    p = P(nc)

    ktmF = ktm.ap().bitcast(F32)
    scaleS = ktmF[:, 0:768]
    scaleR = ktmF[:, 768:1536]
    scaleT = ktmF[:, 1536:1920]

    SX, SG, SUP, SG2, SO, KSG, KA, NMS = 0, 2, 4, 6, 8, 10, 11, 12

    def gstep(fn):
        w = [("gp", p.n("gp"))]
        p.emit("gpsimd", fn, waits=w, inc="gp")

    gstep(lambda g: g.memset(ones_f[:], 1.0))
    gstep(lambda g: g.memset(ones_cb[:], 1.0))
    gstep(lambda g: g.memset(ident[:], 0.0))
    gstep(lambda g: g.affine_select(out=ident[:], in_=ident[:], compare_op=ALU.not_equal,
                                    fill=1.0, base=0, pattern=[[-1, 128]], channel_multiplier=1))
    for i in range(4):
        gstep(lambda g, i=i: g.memset(masks[:, i * TG:(i + 1) * TG], 1.0))
        if i > 0:
            gstep(lambda g, i=i: g.memset(masks[:, i * TG: i * TG + i * 128], 0.0))
        gstep(lambda g, i=i: g.affine_select(
            out=masks[:, i * TG + i * 128: i * TG + (i + 1) * 128],
            in_=masks[:, i * TG + i * 128: i * TG + (i + 1) * 128],
            compare_op=ALU.is_ge, fill=0.0, base=0, pattern=[[1, 128]], channel_multiplier=-1))
    GP_CONST = p.n("gp")

    p.emit("vector", lambda v: v.memset(smax[:], 0.0), inc="ve")
    p.dma("sync", bcol[:, 0:DC], bq_h)
    p.dma("sync", bcol[:, DC:2 * DC], bo_h)
    p.dma("sync", bkcol[:], bk_h)
    p.dma("sync", agm[:], agm_h)
    p.dma("gpsimd", bvrow[:], bv_h, inc="awl")


    def grp(h, tg0, ntok=TG):
        return h.rearrange("(a p) t -> p a t", p=128)[:, :, tg0:tg0 + ntok]

    def buf3(b, nchunk, ntok=TG):
        return b[:, 0:nchunk * ntok].rearrange("p (a t) -> p a t", t=ntok)

    def wprepA(w_h, oc, h, hw, u, red_hist, free0):
        wn = wnat[:, (u % 2) * 1024:(u % 2) * 1024 + hw]
        nblk = hw // 32
        pv = red_hist.get(u - 2, free0)
        p.dma("sync", wn, w_h[oc * 128:(oc + 1) * 128, h * hw:(h + 1) * hw],
              waits=[("ve", pv)], inc="wld")
        lw = p.n("wld")
        p.emit("vector", lambda v, wn=wn, u=u, nblk=nblk: v.tensor_reduce(
            out=scaleR[:, u * nblk:(u + 1) * nblk].rearrange("p (a b) -> p a b", b=1),
            in_=wn.rearrange("p (a b) -> p a b", b=32),
            axis=AX.X, op=ALU.max, apply_absolute_value=True),
            waits=[("wld", lw)], inc="ve")
        red_hist[u] = p.n("ve")

    def wprep_batch(n):
        p.emit("gpsimd", lambda g: g.tensor_scalar(
            out=scaleS[:, 0:n], in0=scaleR[:, 0:n],
            scalar1=1.0 / WQMAX, scalar2=1e-12, op0=ALU.mult, op1=ALU.max),
            waits=[("ve", p.n("ve")), ("gp", p.n("gp"))], inc="gp")
        def f_r0(s):
            s.activation(out=scaleR[:, 0:n], in_=scaleS[:, 0:n], func=ACT.Ln)
            return s.activation(out=scaleR[:, 0:n], in_=scaleR[:, 0:n],
                                func=ACT.Exp, scale=f32(-1.0))
        p.emit("scalar", f_r0, waits=[("gp", p.n("gp"))], inc="ac")
        def f_nw(v):
            r = None
            for hh in range(0, n, 384):
                w = min(384, n - hh)
                v.tensor_tensor(out=scaleT[:, 0:w], in0=scaleS[:, hh:hh + w],
                                in1=scaleR[:, hh:hh + w], op=ALU.mult)
                v.tensor_scalar(out=scaleT[:, 0:w], in0=scaleT[:, 0:w], scalar1=-1.0,
                                scalar2=2.0, op0=ALU.mult, op1=ALU.add)
                r = v.tensor_tensor(out=scaleR[:, hh:hh + w], in0=scaleR[:, hh:hh + w],
                                    in1=scaleT[:, 0:w], op=ALU.mult)
            return r
        p.emit("vector", f_nw, waits=[("ac", p.n("ac"))], inc="ve")
        return p.n("ve")

    def wprepB(w_h, n_out, oc, h, hw, u, arena, tsem, batch_ve, bq_hist, free0,
               arena_guard=None):
        wn = wnat[:, (u % 2) * 1024:(u % 2) * 1024 + hw]
        wq = wnatq[:, (u % 2) * 1024:(u % 2) * 1024 + hw]
        nblk = hw // 32
        pv = bq_hist.get(u - 2, free0)
        p.dma("sync", wn, w_h[oc * 128:(oc + 1) * 128, h * hw:(h + 1) * hw],
              waits=[("ve", pv)], inc="wld")
        lw = p.n("wld")
        def f_dq(v, wn=wn, wq=wq, u=u, nblk=nblk):
            v.tensor_tensor(
                out=wn.rearrange("p (a b) -> p a b", b=32),
                in0=wn.rearrange("p (a b) -> p a b", b=32),
                in1=scaleR[:, u * nblk:(u + 1) * nblk].rearrange("p (a b) -> p a b", b=1)
                    .broadcast_to([128, nblk, 32]),
                op=ALU.mult)
            v.tensor_scalar(out=wn, in0=wn, scalar1=1.0, scalar2=MAGIC,
                            op0=ALU.mult, op1=ALU.add)
            return v.scalar_tensor_tensor(
                out=wq.rearrange("p (a b) -> p a b", b=32),
                in0=wn.rearrange("p (a b) -> p a b", b=32),
                scalar=MAGIC,
                in1=scaleS[:, u * nblk:(u + 1) * nblk].rearrange("p (a b) -> p a b", b=1)
                    .broadcast_to([128, nblk, 32]),
                op0=ALU.subtract, op1=ALU.mult)
        p.emit("vector", f_dq, waits=[("wld", lw), ("ve", batch_ve)], inc="ve")
        vq = p.n("ve")
        bq_hist[u] = vq
        for kk in range(hw // 128):
            kc = h * (hw // 128) + kk
            w = [("ve", vq)] if kk == 0 else []
            if kk == 0 and arena_guard is not None and u == 0:
                w.append(("te", arena_guard))
            p.dma("sync", arena[:, kc * n_out + oc * 128: kc * n_out + (oc + 1) * 128],
                  wq[:, kk * 128:(kk + 1) * 128], waits=w, inc=tsem, transpose=True)

    def wprep_store(arena, store_h, n_out, n_in, tsem):
        wdone = p.n(tsem)
        for kc in range(n_in // 128):
            p.dma("gpsimd", store_h[:, kc * n_out:(kc + 1) * n_out],
                  arena[:, kc * n_out:(kc + 1) * n_out],
                  waits=[(tsem, wdone)] if kc == 0 else [], inc="wst")
        return wdone

    def emit_ln(src_h, mxc0, store, bf_out, tg_hook=None, single_buf=False):
        for tg in range(NTG):
            bo_ = 0 if single_buf else (tg % 2) * 4096
            buf = sIO[:, bo_:bo_ + 4096]
            lsem = "lnl0" if single_buf else f"lnl{tg % 2}"
            g = [("ve2", p.n("ve2"))] if tg >= (1 if single_buf else 2) else []
            p.dma("sync", buf3(buf, DC), grp(src_h, tg * TG), waits=g, inc=lsem)
            lbatch = p.n(lsem)
            for dc in range(DC):
                db = (dc % 2) * TG
                p.emit("vector", lambda v, buf=buf, dc=dc, db=db: v.tensor_copy(
                    out=xbts[:, db:db + TG], in_=buf[:, dc * TG:(dc + 1) * TG]),
                    waits=[(lsem, lbatch), ("te", max(0, p.n("te") - 1))], inc="ve")
                p.emit("scalar", lambda s, buf=buf, dc=dc, db=db: s.activation(
                    out=x2ts[:, db:db + TG], in_=buf[:, dc * TG:(dc + 1) * TG], func=ACT.Square),
                    waits=[(lsem, lbatch), ("te", max(0, p.n("te") - 1))], inc="ac")
                vw, aw = p.n("ve"), p.n("ac")
                def f_mm(t, dc=dc, db=db):
                    t.matmul(out=ps1[:], lhsT=ones_cb[:], rhs=xbts[:, db:db + TG],
                             start=(dc == 0), stop=(dc == DC - 1))
                    return t.matmul(out=ps2[:], lhsT=ones_cb[:], rhs=x2ts[:, db:db + TG],
                                    start=(dc == 0), stop=(dc == DC - 1))
                w = [("ve", vw), ("ac", aw)]
                if dc == 0:
                    w.append(("ve2", p.n("ve2")))
                    w.append(("gp", GP_CONST))
                p.emit("tensor", f_mm, waits=w, inc="te")
            mmw = p.n("te")
            def f_stats(v):
                v.tensor_scalar(out=stat[:, 0:TG], in0=ps1[:], scalar1=1.0 / D, scalar2=None, op0=ALU.mult)
                return v.tensor_scalar(out=stat[:, TG:2 * TG], in0=ps2[:], scalar1=1.0 / D, scalar2=None, op0=ALU.mult)
            p.emit("vector", f_stats, waits=[("te", mmw)], inc="ve2")
            m2t = x2ts.ap().bitcast(F32)[0:1, 0:TG]
            p.emit("vector", lambda v: v.tensor_tensor(out=m2t, in0=stat[:, 0:TG],
                                                       in1=stat[:, 0:TG], op=ALU.mult),
                   waits=[("ve2", p.n("ve2"))], inc="ve2")
            p.emit("vector", lambda v: v.tensor_tensor(out=stat[:, TG:2 * TG], in0=stat[:, TG:2 * TG],
                                                       in1=m2t, op=ALU.subtract),
                   waits=[("ve2", p.n("ve2"))], inc="ve2")
            def f_r(s):
                s.activation(out=stat[:, TG:2 * TG], in_=stat[:, TG:2 * TG], func=ACT.Ln, bias=f32(LN_EPS))
                return s.activation(out=stat[:, TG:2 * TG], in_=stat[:, TG:2 * TG],
                                    func=ACT.Exp, scale=f32(-0.5))
            p.emit("scalar", f_r, waits=[("ve2", p.n("ve2"))], inc="ac")
            def f_mr(v):
                return v.tensor_tensor(out=stat[:, 0:TG], in0=stat[:, 0:TG],
                                       in1=stat[:, TG:2 * TG], op=ALU.mult)
            p.emit("vector", f_mr, waits=[("ac", p.n("ac"))], inc="ve2")
            vw = p.n("ve2")
            def f_bc(t):
                t.matmul(out=pb[2][:], lhsT=ones_f[:], rhs=stat[0:1, TG:2 * TG], start=True, stop=True)
                return t.matmul(out=pb[3][:], lhsT=ones_f[:], rhs=stat[0:1, 0:TG], start=True, stop=True)
            p.emit("tensor", f_bc, waits=[("ve2", vw), ("ac", p.n("ac")), ("gp", GP_CONST)], inc="te")
            tw = p.n("te")
            for dc in range(DC):
                def f_ln(v, buf=buf, dc=dc):
                    v.tensor_tensor(out=sT[:, 0:TG], in0=buf[:, dc * TG:(dc + 1) * TG], in1=pb[2][:], op=ALU.mult)
                    return v.tensor_tensor(out=buf[:, dc * TG:(dc + 1) * TG], in0=sT[:, 0:TG],
                                           in1=pb[3][:], op=ALU.subtract)
                p.emit("vector", f_ln, waits=[("te", tw)], inc="ve2")
                if bf_out:
                    lb0 = 4096 + (tg % 2) * 4096
                    p.emit("vector", lambda v, buf=buf, dc=dc, lb0=lb0: v.tensor_copy(
                        out=sBF[:, lb0 + dc * TG: lb0 + (dc + 1) * TG],
                        in_=buf[:, dc * TG:(dc + 1) * TG]), inc="ve2")
            if mxc0 is not None:
                p.emit("vector", lambda v, buf=buf, tg=tg: v.tensor_reduce(
                    out=mx[:, mxc0 + tg:mxc0 + tg + 1], in_=buf[:, 0:4096],
                    axis=AX.X, op=ALU.max, apply_absolute_value=True), inc="ve2")
            vw2 = p.n("ve2")
            if store:
                p.dma("sync", grp(ln_s, tg * TG), buf3(buf, DC), waits=[("ve2", vw2)], inc="lnst")
            if tg_hook:
                tg_hook(tg, vw2)

    cc_i = [0]

    def emit_scale(vals, extra_sc=None):
        idx = cc_i[0]
        cc_i[0] += 1
        cin, cout = cc_in[idx], cc_out[idx]
        nv = len(vals)
        w = [("ve2", p.n("ve2")), ("gp2", p.n("gp2"))]
        for j, (c0, ncols, _si) in enumerate(vals):
            p.emit("vector", lambda v, c0=c0, ncols=ncols, j=j: v.tensor_reduce(
                out=mx[:, 472 + j:473 + j], in_=mx[:, c0:c0 + ncols], axis=AX.X, op=ALU.max),
                waits=w, inc="ve")
            w = []
        vw = p.n("ve")
        def f_tr(t):
            return t.transpose(out=pk[0:nv, 0:128], in_=mx[:, 472:472 + nv], identity=ident[:])
        p.emit("tensor", f_tr, waits=[("ve", vw), ("ac", p.n("ac")), ("gp", GP_CONST)], inc="te")
        p.emit("vector", lambda v: v.memset(mx[0:4, 505:506], 0.0),
               waits=[("te", p.n("te"))], inc="ve")
        p.emit("vector", lambda v: v.tensor_reduce(out=mx[0:nv, 505:506], in_=pk[0:nv, 0:128],
                                                   axis=AX.X, op=ALU.max),
               waits=[("ve", p.n("ve"))], inc="ve")
        p.dma("gpsimd", cin.ap()[0:4, :], mx[0:4, 505:506], waits=[("ve", p.n("ve"))], inc="ccd")
        def f_cc(g, cin=cin, cout=cout):
            return g.collective_compute("AllReduce", ALU.max, replica_groups=[list(range(NCORE))],
                                        ins=[cin.ap().opt()], outs=[cout.ap().opt()])
        p.emit("gpsimd", f_cc, waits=[("ccd", p.n("ccd"))], inc="cc")
        p.dma("gpsimd", mx[0:4, 506:507], cout.ap(), waits=[("cc", p.n("cc"))], inc="ccd")
        def f_s(v):
            return v.tensor_scalar(out=mx[0:nv, 507:508], in0=mx[0:nv, 506:507],
                                   scalar1=1.0 / QMAX, scalar2=1e-12, op0=ALU.mult, op1=ALU.max)
        p.emit("vector", f_s, waits=[("ccd", p.n("ccd"))], inc="ve")
        vw = p.n("ve")
        def f_tr2(t):
            return t.transpose(out=pk[0:1, 0:nv], in_=mx[0:nv, 507:508], identity=ident[0:nv, 0:nv])
        p.emit("tensor", f_tr2, waits=[("ve", vw)], inc="te")
        def f_row(v):
            v.tensor_copy(out=smax[0:1, 80:80 + nv], in_=pk[0:1, 0:nv])
            v.memset(smax[0:1, 80 + nv:96], 0.0)
            return v.tensor_copy(out=stat[0:1, 0:nv], in_=pk[0:1, 0:nv])
        p.emit("vector", f_row, waits=[("te", p.n("te"))], inc="ve")
        def f_r0(s):
            s.activation(out=stat[0:1, 16:16 + nv], in_=stat[0:1, 0:nv], func=ACT.Ln)
            return s.activation(out=stat[0:1, 16:16 + nv], in_=stat[0:1, 16:16 + nv],
                                func=ACT.Exp, scale=f32(-1.0))
        p.emit("scalar", f_r0, waits=[("ve", p.n("ve"))], inc="ac")
        p.emit("vector", lambda v: v.tensor_tensor(out=stat[0:1, 32:32 + nv], in0=stat[0:1, 0:nv],
                                                   in1=stat[0:1, 16:16 + nv], op=ALU.mult),
               waits=[("ac", p.n("ac"))], inc="ve")
        p.emit("vector", lambda v: v.tensor_scalar(out=stat[0:1, 32:32 + nv], in0=stat[0:1, 32:32 + nv],
                                                   scalar1=-1.0, scalar2=2.0, op0=ALU.mult, op1=ALU.add),
               waits=[("ve", p.n("ve"))], inc="ve")
        p.emit("vector", lambda v: v.tensor_tensor(out=stat[0:1, 16:16 + nv], in0=stat[0:1, 16:16 + nv],
                                                   in1=stat[0:1, 32:32 + nv], op=ALU.mult),
               waits=[("ve", p.n("ve"))], inc="ve")
        p.emit("vector", lambda v: v.tensor_copy(out=smax[0:1, 88:88 + nv], in_=stat[0:1, 16:16 + nv]),
               waits=[("ve", p.n("ve"))], inc="ve")
        vw = p.n("ve")
        def f_bc(t):
            return t.matmul(out=pb[3][:, 0:16], lhsT=ones_f[:], rhs=smax[0:1, 80:96], start=True, stop=True)
        p.emit("tensor", f_bc, waits=[("ve", vw)], inc="te")
        def f_ev(v):
            r = None
            for j, (_c, _n, si) in enumerate(vals):
                v.tensor_copy(out=scol[:, si:si + 1], in_=pb[3][:, j:j + 1])
                r = v.tensor_copy(out=scol[:, si + 1:si + 2], in_=pb[3][:, 8 + j:9 + j])
            return r
        p.emit("vector", f_ev, waits=[("te", p.n("te"))], inc="ve")
        if extra_sc is not None:
            p.emit("vector", extra_sc, waits=[("ve", p.n("ve"))], inc="ve")
        return p.n("ve")

    def emit_bigmm(rh_s, n_k, warena, n_out, dst_s, mxc0, first_waits, tg_hook=None,
                   rhs_after=None, lhs_after=None, pb_free=None):
        nkc = n_k // 128
        noc = n_out // 128
        rw = nkc * TG
        te_after = {}
        evac_ac = {}
        if pb_free is None:
            pb_free = p.n("ac")
        for tg in range(NTG):
            rb0 = (tg % 2) * rw
            lsem = f"mml{tg % 2}"
            g = [("te", te_after[tg - 2])] if tg >= 2 else (list(first_waits) + [("te", p.n("te"))])
            if rhs_after is not None:
                g = g + [rhs_after[tg]]
            p.dma("sync", buf3(sBF[:, rb0:rb0 + rw], nkc), grp(rh_s, tg * TG),
                  waits=g, inc=lsem)
            lbatch = p.n(lsem)
            for oc in range(noc):
                k = tg * noc + oc
                bank = k % 4
                w = [(lsem, lbatch),
                     ("ac", evac_ac[k - 4] if k >= 4 else pb_free)]
                if lhs_after is not None:
                    if tg == 0:
                        w += lhs_after(oc)
                elif tg == 0 and oc == 0:
                    w += list(first_waits)
                def f_mm(t, oc=oc, rb0=rb0, bank=bank):
                    r = None
                    for kc in range(nkc):
                        r = t.matmul(out=pb[bank][:],
                                     lhsT=warena[:, kc * n_out + oc * 128: kc * n_out + (oc + 1) * 128],
                                     rhs=sBF[:, rb0 + kc * TG: rb0 + (kc + 1) * TG],
                                     start=(kc == 0), stop=(kc == nkc - 1))
                    return r
                p.emit("tensor", f_mm, waits=w, inc="te")
                tw = p.n("te")
                ev = k % 4
                ssem = f"mst{ev}"
                def f_ev(s, bank=bank, ev=ev):
                    return s.copy(out=sT[:, 2048 + ev * TG:2048 + (ev + 1) * TG], in_=pb[bank][:])
                p.emit("scalar", f_ev, waits=[("te", tw), (ssem, p.n(ssem))], inc="ac")
                aw = p.n("ac")
                evac_ac[k] = aw
                p.emit("vector", lambda v, ev=ev, c=mxc0 + k % 192: v.tensor_reduce(
                    out=mx[:, c:c + 1], in_=sT[:, 2048 + ev * TG:2048 + (ev + 1) * TG], axis=AX.X,
                    op=ALU.max, apply_absolute_value=True), waits=[("ac", aw)], inc="ve")
                p.dma("gpsimd", dst_s[oc * 128:(oc + 1) * 128, tg * TG:(tg + 1) * TG],
                      sT[:, 2048 + ev * TG:2048 + (ev + 1) * TG], waits=[("ac", aw)], inc=ssem)
            te_after[tg] = p.n("te")
            if tg_hook:
                tg_hook(tg)

    g2_hist = {"ve2": {}, "ac": {}, "os": {}}

    def g2_unit(oc, ci, idx, dpass, mxg2c=None, mxoc=None, first_waits=()):
        W = 512
        par = idx % 4
        gin = sIO[:, par * W: par * W + W]
        tA = sT[:, par * W: par * W + W]
        tB = sIO[:, 6144 + par * W: 6144 + par * W + W]
        if dpass:
            upin = sIO[:, 2048 + par * W: 2048 + par * W + W]
            oout = sIO[:, 4096 + par * W: 4096 + par * W + W]
        lsem = f"gl{par % 2}"
        prev = g2_hist["ve2"].get(idx - 4)
        g = ([("ve2", prev)] if prev is not None else list(first_waits))
        ld1 = p.dma("sync", gin,
                    gate_s[oc * 128:(oc + 1) * 128, ci * W:(ci + 1) * W],
                    waits=g, inc=lsem)
        if dpass:
            p.dma("sync", upin,
                  up_s[oc * 128:(oc + 1) * 128, ci * W:(ci + 1) * W], inc=lsem)
        lbatch = p.n(lsem)
        def f_t1(s):
            return s.activation(out=tA, in_=gin, func=ACT.Identity,
                                bias=f32(MAGIC), scale=scol[:, SG + 1:SG + 2])
        p.emit("scalar", f_t1, waits=[(lsem, lbatch)], inc="ac")
        def f_sig(s):
            return s.activation(out=tB, in_=tA, func=ACT.Sigmoid,
                                bias=scol[:, NMS:NMS + 1], scale=scol[:, SG:SG + 1])
        p.emit("scalar", f_sig, inc="ac")
        acw = p.n("ac")
        t4w = None
        if dpass:
            osem = f"os{par % 2}"
            oprev = g2_hist["os"].get(idx - 4)
            def f_t4(s):
                return s.activation(out=oout, in_=upin,
                                    func=ACT.Identity, bias=f32(MAGIC), scale=scol[:, SUP + 1:SUP + 2])
            p.emit("scalar", f_t4, waits=([(osem, oprev)] if oprev is not None else []), inc="ac")
            t4w = p.n("ac")
        def f_g2(v):
            v.tensor_scalar(out=tB, in0=tB, scalar1=65536.0, scalar2=MAGIC,
                            op0=ALU.mult, op1=ALU.add)
            v.tensor_scalar(out=tB, in0=tB, scalar1=MAGIC, scalar2=65535.0,
                            op0=ALU.subtract, op1=ALU.min)
            v.tensor_scalar(out=gin, in0=tA, scalar1=MAGIC, scalar2=scol[:, KSG:KSG + 1],
                            op0=ALU.subtract, op1=ALU.mult)
            r = v.tensor_tensor(out=tB, in0=gin, in1=tB, op=ALU.mult)
            if not dpass:
                r = v.tensor_reduce(out=mx[:, mxg2c:mxg2c + 1], in_=tB,
                                    axis=AX.X, op=ALU.max, apply_absolute_value=True)
            else:
                v.tensor_scalar(out=tA, in0=tB, scalar1=scol[:, SG2 + 1:SG2 + 2], scalar2=MAGIC,
                                op0=ALU.mult, op1=ALU.add)
                r = v.tensor_scalar(out=tB, in0=tA, scalar1=MAGIC, scalar2=scol[:, KA:KA + 1],
                                    op0=ALU.subtract, op1=ALU.mult)
            return r
        p.emit("vector", f_g2, waits=[("ac", acw)], inc="ve2")
        if not dpass:
            g2_hist["ve2"][idx] = p.n("ve2")
        if dpass:
            def f_o(v):
                return v.scalar_tensor_tensor(out=oout, in0=oout,
                                              scalar=MAGIC, in1=tB, op0=ALU.subtract, op1=ALU.mult)
            p.emit("vector", f_o, waits=[("ac", t4w)], inc="ve2")
            p.emit("vector", lambda v, c=mxoc: v.tensor_reduce(
                out=mx[:, c:c + 1], in_=oout,
                axis=AX.X, op=ALU.max, apply_absolute_value=True), inc="ve2")
            g2_hist["ve2"][idx] = p.n("ve2")
            p.dma("gpsimd", o_sc[oc * 128:(oc + 1) * 128, ci * W:(ci + 1) * W],
                  oout, waits=[("ve2", p.n("ve2"))], inc=osem)
            g2_hist["os"][idx] = p.n(osem)

    def emit_ffn(src_h, res_h, dst_h, phase, dst_sem):
        MXLN, MXG, MXUP, MXG2, MXO = 200, 0, 0, 192, 192
        wtg_base = p.n("wtg")
        wtu_base = p.n("wtu")
        ffn_ac0 = p.n("ac")
        wuB_bve = [0]
        wuB_bq = {}
        if phase == 0:
            redg = {}
            for oc in range(OC):
                wprepA(wg_h, oc, 0, D, oc, redg, 0)
            bveg = wprep_batch(768)
            bqg = {}
            fg = p.n("ve")
            for oc in range(OC):
                wprepB(wg_h, I, oc, 0, D, oc, wA, "wtg", bveg, bqg, fg)
            wg_done = wprep_store(wA, wgT_s, I, D, "wtg")
            redu = {}
            for oc in range(OC):
                wprepA(wu_h, oc, 0, D, oc, redu, bqg[OC - 1])
            wuB_bve[0] = wprep_batch(768)
        emit_ln(src_h, MXLN, store=True, bf_out=False)
        emit_scale([(MXLN, NTG, SX)])
        sx_done = p.n("ve")

        if phase == 0:
            pass
        else:
            wgl = [("te", p.n("te")), ("agd", p.n("agd")), ("wst", p.n("wst"))]
            for kc in range(DC):
                p.dma("sync", wA[:, kc * I:(kc + 1) * I], wgT_s[:, kc * I:(kc + 1) * I],
                      waits=(wgl if kc == 0 else []), inc="wtg")
            wg_done = p.n("wtg")

        lnst_done = p.n("lnst")
        xq_after = {}
        xq_ve2 = {}
        for tg in range(NTG):
            bo_ = (tg % 2) * 4096
            buf = sIO[:, bo_:bo_ + 4096]
            lsem = f"xql{tg % 2}"
            g = [("ve2", xq_ve2[tg - 2])] if tg >= 2 else [("lnst", lnst_done)]
            p.dma("sync", buf3(buf, DC), grp(ln_s, tg * TG), waits=g, inc=lsem)
            lbatch = p.n(lsem)
            for dc in range(DC):
                fw = [(lsem, lbatch)]
                if dc == 0:
                    if tg == 0:
                        fw.append(("ve", sx_done))
                    if tg >= 1:
                        fw.append(("xqst", xq_after[tg - 1]))
                def f_q(v, buf=buf, dc=dc):
                    v.tensor_scalar(out=sT[:, 0:TG], in0=buf[:, dc * TG:(dc + 1) * TG],
                                    scalar1=scol[:, SX + 1:SX + 2], scalar2=MAGIC, op0=ALU.mult, op1=ALU.add)
                    return v.tensor_scalar(out=sBF[:, 8192 + dc * TG:8192 + (dc + 1) * TG],
                                           in0=sT[:, 0:TG], scalar1=MAGIC, scalar2=scol[:, SX:SX + 1],
                                           op0=ALU.subtract, op1=ALU.mult)
                p.emit("vector", f_q, waits=fw, inc="ve2")
            vq = p.n("ve2")
            xq_ve2[tg] = vq
            p.dma("sync", grp(xq_s, tg * TG), buf3(sBF[:, 8192:12288], DC),
                  waits=[("ve2", vq)], inc="xqst")
            xq_after[tg] = p.n("xqst")
        xq_done = p.n("xqst")

        if phase == 0:
            upg = (OC + NTG - 1) // NTG
            wuB_f = [None]
            def b1hook(tg):
                if wuB_f[0] is None:
                    wuB_f[0] = p.n("ve")
                for j in range(upg):
                    oc = tg * upg + j
                    if oc < OC:
                        wprepB(wu_h, I, oc, 0, D, oc, wB, "wtu", wuB_bve[0], wuB_bq,
                               wuB_f[0])
        else:
            wul = [("te", p.n("te"))]
            for kc in range(DC):
                p.dma("sync", wB[:, kc * I:(kc + 1) * I], wuT_s[:, kc * I:(kc + 1) * I],
                      waits=(wul if kc == 0 else []), inc="wtu")
            b1hook = None
        if phase == 0:
            lhs_g = lambda oc: [("wtg", wtg_base + 128 * (oc + 1))]
        else:
            lhs_g = lambda oc: [("wtg", wg_done)]
        emit_bigmm(xq_s, D, wA, I, gate_s, MXG, [], tg_hook=b1hook,
                   rhs_after=[("xqst", xq_after[tg]) for tg in range(NTG)], lhs_after=lhs_g,
                   pb_free=ffn_ac0)
        b1_te = p.n("te")
        ac_after_b1 = p.n("ac")
        if phase == 0:
            redd = {}
            fd = wuB_bq[OC - 1]
            for oc in range(DC):
                for h in range(3):
                    wprepA(wd_h, oc, h, 1024, oc * 3 + h, redd, fd)
            bved = wprep_batch(768)
            bqd = {}
            fdB = p.n("ve")
            for oc in range(DC):
                for h in range(3):
                    wprepB(wd_h, D, oc, h, 1024, oc * 3 + h, wA, "wtd", bved, bqd, fdB,
                           arena_guard=b1_te)
            wd_done = wprep_store(wA, wdT_s, D, I, "wtd")
        if phase == 0:
            wu_done = wprep_store(wB, wuT_s, I, D, "wtu")
        else:
            wu_done = p.n("wtu")
        gate_stored = [p.n(f"mst{i}") for i in range(4)]
        def mk_ksg(v):
            v.tensor_scalar(out=scol[:, KSG:KSG + 1], in0=scol[:, SG:SG + 1],
                            scalar1=1.0 / 65536.0, scalar2=None, op0=ALU.mult)
            return v.tensor_scalar(out=scol[:, NMS:NMS + 1], in0=scol[:, SG:SG + 1],
                                   scalar1=-MAGIC, scalar2=None, op0=ALU.mult)
        emit_scale([(MXG, min(192, NTG * OC), SG)], extra_sc=mk_ksg)
        sg_done = p.n("ve")

        idx = [0]
        chain_first = [("ve", sg_done)] + [(f"mst{i}", gate_stored[i]) for i in range(4)]
        def b2hook(tg):
            for j in range(2 * NCH * OC // NTG):
                k = idx[0]
                g2_unit(k // (2 * NCH), k % (2 * NCH) if False else (k % (2 * NCH)) // 2, k,
                        dpass=False, mxg2c=MXG2 + k % 96,
                        first_waits=chain_first) if False else None
                idx[0] += 1
        idx = [0]
        nci = TOK // 512
        g2_hist["ve2"].clear(); g2_hist["os"].clear()
        def b2hook(tg):
            for j in range(OC * nci // NTG):
                k = idx[0]
                g2_unit(k // nci, k % nci, k, dpass=False, mxg2c=MXG2 + k % 384,
                        first_waits=chain_first)
                idx[0] += 1
        if phase == 0:
            lhs_u = lambda oc: [("wtu", wtu_base + 128 * (oc + 1))]
        else:
            lhs_u = lambda oc: [("wtu", wu_done)]
        emit_bigmm(xq_s, D, wB, I, up_s, MXUP, [], tg_hook=b2hook,
                   rhs_after=[("xqst", xq_after[tg]) for tg in range(NTG)], lhs_after=lhs_u,
                   pb_free=ac_after_b1)
        up_mm_done = p.n("te")
        up_stored = [p.n(f"mst{i}") for i in range(4)]
        def mk_ka(v):
            return v.tensor_tensor(out=scol[:, KA:KA + 1], in0=scol[:, SG2:SG2 + 1],
                                   in1=scol[:, SUP:SUP + 1], op=ALU.mult)
        emit_scale([(MXUP, min(192, NTG * OC), SUP), (MXG2, min(384, OC * (TOK // 512)), SG2)], extra_sc=mk_ka)
        sup_done = p.n("ve")

        if phase == 0:
            pass
        else:
            for kc in range(OC):
                p.dma("sync", wA[:, kc * D:(kc + 1) * D], wdT_s[:, kc * D:(kc + 1) * D],
                      waits=[("te", up_mm_done)] if kc == 0 else [], inc="wtd")
            wd_done = p.n("wtd")

        d_first = [("ve", sup_done)] + [(f"mst{i}", up_stored[i]) for i in range(4)]
        g2_hist["ve2"].clear(); g2_hist["os"].clear()
        ndci = TOK // 512
        for k in range(OC * ndci):
            g2_unit(k // ndci, k % ndci, k, dpass=True, mxoc=MXO + k % 192, first_waits=d_first)
        emit_scale([(MXO, min(192, OC * (TOK // 512)), SO)])
        so_done = p.n("ve")
        o_stored = [p.n(f"os{i}") for i in range(2)]

        e_te = {}
        e_oq = {}
        HT = 256
        for htg in range(NTG * 2):
            tg, half = htg // 2, htg % 2
            t0 = tg * TG + half * HT
            ob = (htg % 2) * 6144
            if htg == 0:
                g = [(f"os{i}", o_stored[i]) for i in range(2)] + [("wtd", wd_done), ("ve", so_done)]
            else:
                g = [("ve2", e_oq[htg - 1])]
            p.dma("sync", buf3(sIO[:, 0:6144], OC, HT), grp(o_sc, t0, HT), waits=g, inc="el0")
            lb = p.n("el0")
            def f_oq(v, ob=ob):
                v.tensor_scalar(out=sIO[:, 6144:8192].bitcast(BF16)[:, 0:4096] if False else sT[:, 0:2048],
                                in0=sIO[:, 0:2048], scalar1=scol[:, SO + 1:SO + 2], scalar2=MAGIC,
                                op0=ALU.mult, op1=ALU.add)
                v.tensor_scalar(out=sBF[:, ob:ob + 2048], in0=sT[:, 0:2048],
                                scalar1=MAGIC, scalar2=scol[:, SO:SO + 1], op0=ALU.subtract, op1=ALU.mult)
                v.tensor_scalar(out=sT[:, 0:2048], in0=sIO[:, 2048:4096],
                                scalar1=scol[:, SO + 1:SO + 2], scalar2=MAGIC, op0=ALU.mult, op1=ALU.add)
                v.tensor_scalar(out=sBF[:, ob + 2048:ob + 4096], in0=sT[:, 0:2048],
                                scalar1=MAGIC, scalar2=scol[:, SO:SO + 1], op0=ALU.subtract, op1=ALU.mult)
                v.tensor_scalar(out=sT[:, 0:2048], in0=sIO[:, 4096:6144],
                                scalar1=scol[:, SO + 1:SO + 2], scalar2=MAGIC, op0=ALU.mult, op1=ALU.add)
                return v.tensor_scalar(out=sBF[:, ob + 4096:ob + 6144], in0=sT[:, 0:2048],
                                       scalar1=MAGIC, scalar2=scol[:, SO:SO + 1], op0=ALU.subtract, op1=ALU.mult)
            w = [("el0", lb)]
            if htg >= 2:
                w.append(("te", e_te[htg - 2]))
            p.emit("vector", f_oq, waits=w, inc="ve2")
            e_oq[htg] = p.n("ve2")
            vq = p.n("ve2")
            p.dma("sync", buf3(sIO[:, 6144:8192], DC, HT), grp(res_h, t0, HT),
                  waits=[("ve2", max(0, p.n("ve2") - 1))], inc="rl")
            rlb = p.n("rl")
            for dc in range(DC):
                bank = dc % 4
                w = [("rl", rlb), ("ve2", vq if dc < 4 else max(0, p.n("ve2") - 3))]
                def f_mm(t, dc=dc, bank=bank, ob=ob):
                    r = None
                    for ic in range(OC):
                        r = t.matmul(out=pb[bank][:, 0:HT], lhsT=wA[:, ic * D + dc * 128: ic * D + (dc + 1) * 128],
                                     rhs=sBF[:, ob + ic * HT: ob + (ic + 1) * HT], start=(ic == 0), stop=(ic == OC - 1))
                    return r
                p.emit("tensor", f_mm, waits=w, inc="te")
                tw = p.n("te")
                dsem = f"{dst_sem}{dc % 4}"
                def f_ep(v, dc=dc, bank=bank):
                    return v.scalar_tensor_tensor(out=sT[:, 2048 + (dc % 4) * TG: 2048 + (dc % 4) * TG + HT],
                                                  in0=pb[bank][:, 0:HT], scalar=0.5,
                                                  in1=sIO[:, 6144 + dc * HT:6144 + (dc + 1) * HT],
                                                  op0=ALU.mult, op1=ALU.add)
                p.emit("vector", f_ep, waits=[("te", tw), (dsem, p.n(dsem))], inc="ve2")
                p.dma("gpsimd", dst_h[dc * 128:(dc + 1) * 128, t0:t0 + HT],
                      sT[:, 2048 + (dc % 4) * TG: 2048 + (dc % 4) * TG + HT],
                      waits=[("ve2", p.n("ve2"))], inc=dsem)
            e_te[htg] = p.n("te")

    def emit_attn():
        kfm = sIO.ap().bitcast(BF16)[:, 12288:16384]
        vtm = sT.ap().bitcast(BF16)[:, 4096:8192]
        atm = sIO.ap().bitcast(BF16)[:, 10240:12288]
        kvf = wA.ap().bitcast(F32)[:, 0:8192]
        kvb = wA[:, 16384:24576]

        for dc in range(DC):
            p.dma("gpsimd", wB[:, dc * D:(dc + 1) * D], wqT_h[dc * 128:(dc + 1) * 128, :],
                  waits=[("te", p.n("te"))] if dc == 0 else [], inc="awl")
            p.dma("gpsimd", wB[:, 8192 + dc * D: 8192 + (dc + 1) * D], wkT_h[dc * 128:(dc + 1) * 128, :], inc="awl")
            p.dma("gpsimd", wB[:, 16384 + dc * D: 16384 + (dc + 1) * D], wvT_h[dc * 128:(dc + 1) * 128, :], inc="awl")
        awl_done = p.n("awl")
        zw = [("te", p.n("te")), ("ve2", p.n("ve2")), ("d_sync", p.n("d_sync")), ("d_gpsimd", p.n("d_gpsimd"))]
        p.emit("gpsimd", lambda g: g.memset(kvf, 0.0), waits=zw, inc="gp")
        p.emit("gpsimd", lambda g: g.memset(kvb[:, :], 0.0), waits=[("gp", p.n("gp"))], inc="gp")
        p.emit("gpsimd", lambda g: g.memset(mx[:, 0:16], 0.0), waits=[("gp", p.n("gp"))], inc="gp")
        p.emit("gpsimd", lambda g: g.memset(ksb[:, :], 0.0), waits=[("gp", p.n("gp"))], inc="gp")
        kv_zero = p.n("gp")

        def stage_a(sc, lnw):
            first = (sc == 0)
            LB = 4096 + (sc % 2) * 4096
            hist = {}
            for pj in range(2 * DC):
                isq = pj < DC
                dco = pj % DC
                bank = pj % 4
                warr = 0 if isq else 8192
                w = []
                pv = hist.get(pj - 4)
                if pv is not None:
                    w = [("ve2", pv)]
                elif first and pj == 0:
                    w = [("ve2", lnw), ("awl", awl_done), ("gp", kv_zero)]
                else:
                    w = [("ve2", max(0, p.n("ve2") - 2))]
                def f_pmm(t, dco=dco, bank=bank, warr=warr):
                    r = None
                    for dc in range(DC):
                        r = t.matmul(out=pb[bank][:], lhsT=wB[:, warr + dc * D + dco * 128: warr + dc * D + (dco + 1) * 128],
                                     rhs=sBF[:, LB + dc * TG: LB + (dc + 1) * TG],
                                     start=(dc == 0), stop=(dc == DC - 1))
                    return r
                p.emit("tensor", f_pmm, waits=w, inc="te")
                tw = p.n("te")
                bc = bcol if isq else bkcol
                bi = dco if isq else dco
                tsl = sT[:, (pj % 2) * 2 * TG:(pj % 2) * 2 * TG + TG]
                tsl2 = sT[:, (pj % 2) * 2 * TG + TG:(pj % 2) * 2 * TG + 2 * TG]
                def f_ab(v, bank=bank, bc=bc, bi=bi, tsl=tsl, tsl2=tsl2):
                    v.tensor_scalar(out=tsl, in0=pb[bank][:], scalar1=bc[:, bi:bi + 1],
                                    scalar2=0.0, op0=ALU.add, op1=ALU.max)
                    return v.tensor_scalar(out=tsl2, in0=pb[bank][:], scalar1=bc[:, bi:bi + 1],
                                           scalar2=0.0, op0=ALU.add, op1=ALU.min)
                p.emit("vector", f_ab, waits=[("te", tw)], inc="ve2")
                p.emit("scalar", lambda s, tsl2=tsl2: s.activation(out=tsl2, in_=tsl2, func=ACT.Exp),
                       waits=[("ve2", p.n("ve2"))], inc="ac")
                dst = sBF[:, dco * TG:(dco + 1) * TG] if isq else kfm[:, dco * TG:(dco + 1) * TG]
                def f_po(v, dst=dst, tsl=tsl, tsl2=tsl2):
                    return v.tensor_tensor(out=dst, in0=tsl, in1=tsl2, op=ALU.add)
                qw = [("ac", p.n("ac"))]
                if isq:
                    qw.append((f"qst{(sc + 1) % 2}", p.n(f"qst{(sc + 1) % 2}")))
                p.emit("vector", f_po, waits=qw, inc="ve2")
                hist[pj] = p.n("ve2")
                if isq:
                    p.dma("gpsimd", q_s[dco * 128:(dco + 1) * 128, sc * TG:(sc + 1) * TG],
                          sBF[:, dco * TG:(dco + 1) * TG], waits=[("ve2", p.n("ve2"))], inc=f"qst{sc % 2}")
            k_done = p.n("ve2")
            for tc in range(4):
                for dc in range(DC):
                    w = []
                    if tc == 0 and dc == 0:
                        w = [("ve2", k_done), ("te", p.n("te"))]
                    p.dma("sync", ktm[:, tc * 1024 + dc * 128: tc * 1024 + (dc + 1) * 128],
                          kfm[:, dc * TG + tc * 128: dc * TG + (tc + 1) * 128],
                          waits=w, inc="ktr", transpose=True)
            ktr_done = p.n("ktr")
            for tc in range(4):
                def f_vmm(t, tc=tc):
                    r = None
                    for dc in range(DC):
                        for hh in range(2):
                            r = t.matmul(out=pk[:, hh * 512:(hh + 1) * 512],
                                         lhsT=sBF[:, LB + dc * TG + tc * 128: LB + dc * TG + (tc + 1) * 128],
                                         rhs=wB[:, 16384 + dc * D + hh * 512: 16384 + dc * D + (hh + 1) * 512],
                                         start=(dc == 0), stop=(dc == DC - 1))
                    return r
                p.emit("tensor", f_vmm, waits=[("ve2", p.n("ve2"))], inc="te")
                tw = p.n("te")
                def f_vo(v, tc=tc):
                    return v.tensor_tensor(out=vtm[:, tc * 1024:(tc + 1) * 1024], in0=pk[:],
                                           in1=bvrow[:, :], op=ALU.add)
                p.emit("vector", f_vo, waits=[("te", tw)], inc="ve2")
            vtm_done = p.n("ve2")
            mhist = {}
            for i in range(4):
                ncq = TG - i * 128
                bank = i % 4
                def f_at(t, i=i, bank=bank, ncq=ncq):
                    r = None
                    for dc in range(DC):
                        r = t.matmul(out=pb[bank][:, 0:ncq],
                                     lhsT=kfm[:, dc * TG + i * 128: dc * TG + (i + 1) * 128],
                                     rhs=sBF[:, dc * TG + i * 128: dc * TG + i * 128 + ncq],
                                     start=(dc == 0), stop=(dc == DC - 1))
                    return r
                p.emit("tensor", f_at, waits=[("ve2", vtm_done if i == 0 else p.n("ve2"))], inc="te")
                tw = p.n("te")
                def f_mask(v, i=i, bank=bank, ncq=ncq):
                    return v.tensor_tensor(out=atm[:, i * TG + i * 128: (i + 1) * TG],
                                           in0=pb[bank][:, 0:ncq],
                                           in1=masks[:, i * TG + i * 128:(i + 1) * TG], op=ALU.mult)
                p.emit("vector", f_mask, waits=[("te", tw), ("gp", GP_CONST)], inc="ve2")
            at_done = p.n("ve2")
            def f_den(t):
                t.matmul(out=ps1[:], lhsT=ones_cb[:], rhs=atm[:, 0:TG], start=True, stop=False)
                for i in range(1, 4):
                    t.matmul(out=ps1[0:1, i * 128:TG], lhsT=ones_cb[:],
                             rhs=atm[:, i * TG + i * 128:(i + 1) * TG], start=False, stop=False)
                r = None
                for dc in range(DC):
                    r = t.matmul(out=ps1[:], lhsT=ksb[:, dc:dc + 1], rhs=sBF[:, dc * TG:(dc + 1) * TG],
                                 start=False, stop=(dc == DC - 1))
                return r
            p.emit("tensor", f_den, waits=[("ve2", at_done)], inc="te")
            tw = p.n("te")
            p.emit("vector", lambda v: v.tensor_copy(out=stat[:, 0:TG], in_=ps1[:]),
                   waits=[("te", tw)], inc="ve2")
            p.dma("sync", den_s[0:1, sc * TG:(sc + 1) * TG], stat[0:1, 0:TG],
                  waits=[("ve2", p.n("ve2"))], inc="dst")
            ehist = {}
            for ec in range(DC):
                bank = ec % 4
                nsem = f"nst{ec % 2}"
                w = [("ktr", ktr_done), ("ac", max(0, p.n("ac") - 3))]
                if ec == 0:
                    w.append(("ve2", at_done))
                def f_num(t, ec=ec, bank=bank):
                    t.matmul(out=pb[bank][:], lhsT=vtm[:, 0 * 1024 + ec * 128: 0 * 1024 + (ec + 1) * 128],
                             rhs=atm[:, 0:TG], start=True, stop=False)
                    for i in range(1, 4):
                        t.matmul(out=pb[bank][0:128, i * 128:TG],
                                 lhsT=vtm[:, i * 1024 + ec * 128: i * 1024 + (ec + 1) * 128],
                                 rhs=atm[:, i * TG + i * 128:(i + 1) * TG], start=False, stop=False)
                    r = None
                    for dc in range(DC):
                        r = t.matmul(out=pb[bank][:], lhsT=kvb[:, dc * 1024 + ec * 128: dc * 1024 + (ec + 1) * 128],
                                     rhs=sBF[:, dc * TG:(dc + 1) * TG], start=False, stop=(dc == DC - 1))
                    return r
                p.emit("tensor", f_num, waits=w, inc="te")
                tw = p.n("te")
                ev = ec % 2
                def f_ev(s, bank=bank, ev=ev):
                    return s.copy(out=sT[:, 1024 + ev * TG:1024 + (ev + 1) * TG], in_=pb[bank][:])
                p.emit("scalar", f_ev, waits=[("te", tw), (nsem, p.n(nsem))], inc="ac")
                p.dma("gpsimd", num_s[ec * 128:(ec + 1) * 128, sc * TG:(sc + 1) * TG],
                      sT[:, 1024 + ev * TG:1024 + (ev + 1) * TG], waits=[("ac", p.n("ac"))], inc=nsem)
            for dc in range(DC):
                usepk = (dc % 2 == 0)
                w = [("ktr", ktr_done)]
                pv = ehist.get(dc - 2)
                w.append(("ve2", pv if pv is not None else p.n("ve2")))
                def f_fold(t, dc=dc, usepk=usepk):
                    r = None
                    for tc in range(4):
                        for hh in range(2):
                            out = pk[:, hh * 512:(hh + 1) * 512] if usepk else pb[2 + hh][:]
                            r = t.matmul(out=out,
                                         lhsT=ktm[:, tc * 1024 + dc * 128: tc * 1024 + (dc + 1) * 128],
                                         rhs=vtm[:, tc * 1024 + hh * 512: tc * 1024 + (hh + 1) * 512],
                                         start=(tc == 0), stop=(tc == 3))
                    return r
                p.emit("tensor", f_fold, waits=w, inc="te")
                tw = p.n("te")
                def f_acc(v, dc=dc, usepk=usepk):
                    if usepk:
                        v.tensor_tensor(out=kvf[:, dc * 1024:(dc + 1) * 1024], in0=kvf[:, dc * 1024:(dc + 1) * 1024],
                                        in1=pk[:], op=ALU.add)
                    else:
                        v.tensor_tensor(out=kvf[:, dc * 1024:dc * 1024 + 512], in0=kvf[:, dc * 1024:dc * 1024 + 512],
                                        in1=pb[2][:], op=ALU.add)
                        v.tensor_tensor(out=kvf[:, dc * 1024 + 512:(dc + 1) * 1024], in0=kvf[:, dc * 1024 + 512:(dc + 1) * 1024],
                                        in1=pb[3][:], op=ALU.add)
                    return v.tensor_copy(out=kvb[:, dc * 1024:(dc + 1) * 1024],
                                         in_=kvf[:, dc * 1024:(dc + 1) * 1024])
                p.emit("vector", f_acc, waits=[("te", tw)], inc="ve2")
                ehist[dc] = p.n("ve2")
            def f_ks(t):
                r = None
                for dc in range(DC):
                    for tc in range(4):
                        r = t.matmul(out=pb[0][:, dc:dc + 1],
                                     lhsT=ktm[:, tc * 1024 + dc * 128: tc * 1024 + (dc + 1) * 128],
                                     rhs=ones_cb[:], start=(tc == 0), stop=(tc == 3))
                return r
            p.emit("tensor", f_ks, waits=[("ve2", max(0, p.n("ve2") - DC))], inc="te")
            tw = p.n("te")
            p.emit("vector", lambda v: v.tensor_tensor(out=mx[:, 0:DC], in0=mx[:, 0:DC],
                                                       in1=pb[0][:, 0:DC], op=ALU.add),
                   waits=[("te", tw)], inc="ve2")
            p.emit("vector", lambda v: v.tensor_copy(out=ksb[:, 0:DC], in_=mx[:, 0:DC]),
                   waits=[("ve2", p.n("ve2"))], inc="ve2")

        emit_ln(x1_s, None, store=False, bf_out=True, tg_hook=stage_a, single_buf=True)

        kvw = p.n("ve2")
        for dc in range(DC):
            p.dma("gpsimd", ag_in.ap()[:, dc * D:(dc + 1) * D], kvf[:, dc * 1024:(dc + 1) * 1024],
                  waits=[("ve2", kvw)] if dc == 0 else [], inc="agd")
        p.dma("gpsimd", ag_in.ap()[:, DC * D:DC * D + 8], mx[:, 0:8], inc="agd")
        def f_ag(g):
            return g.collective_compute("AllGather", ALU.bypass,
                                        replica_groups=[[0, 1, 2, 3], [4, 5, 6, 7]],
                                        ins=[ag_in.ap().opt()], outs=[ag_out.ap().opt()])
        p.emit("gpsimd", f_ag, waits=[("agd", p.n("agd"))], inc="cc")
        ag_done = p.n("cc")

        p.emit("gpsimd", lambda g: g.memset(kvf, 0.0),
               waits=[("cc", ag_done), ("d_sync", p.n("d_sync"))], inc="gp")
        p.emit("gpsimd", lambda g: g.memset(mx[:, 8:16], 0.0), waits=[("gp", p.n("gp"))], inc="gp")
        gpz = p.n("gp")
        for j in range(4):
            for dc in range(DC):
                ld = p.dma("sync", sIO[:, 0:1024], ag_out.ap()[j * 128:(j + 1) * 128, dc * D:(dc + 1) * D],
                           waits=[("ve2", p.n("ve2"))] + ([("gp", gpz)] if (j == 0 and dc == 0) else []), inc="agl")
                def f_acc(v, j=j, dc=dc):
                    return v.scalar_tensor_tensor(out=kvf[:, dc * 1024:(dc + 1) * 1024],
                                                  in0=sIO[:, 0:1024], scalar=agm[:, j:j + 1],
                                                  in1=kvf[:, dc * 1024:(dc + 1) * 1024],
                                                  op0=ALU.mult, op1=ALU.add)
                p.emit("vector", f_acc, waits=[("agl", ld)], inc="ve2")
            ld = p.dma("sync", sIO[:, 0:DC], ag_out.ap()[j * 128:(j + 1) * 128, DC * D:DC * D + DC],
                       waits=[("ve2", p.n("ve2"))], inc="agl")
            def f_ksi(v, j=j):
                return v.scalar_tensor_tensor(out=mx[:, 8:8 + DC], in0=sIO[:, 0:DC],
                                              scalar=agm[:, j:j + 1], in1=mx[:, 8:8 + DC],
                                              op0=ALU.mult, op1=ALU.add)
            p.emit("vector", f_ksi, waits=[("agl", ld), ("ve2", p.n("ve2"))], inc="ve2")
        for dc in range(DC):
            p.emit("vector", lambda v, dc=dc: v.tensor_copy(
                out=wB[:, 8192 + dc * 1024: 8192 + (dc + 1) * 1024],
                in_=kvf[:, dc * 1024:(dc + 1) * 1024]),
                waits=[("awl", p.n("awl"))] if dc == 0 else [], inc="ve2")
        p.emit("vector", lambda v: v.tensor_copy(out=ksb[:, 8:16], in_=mx[:, 8:16]),
               waits=[("ve2", p.n("ve2"))], inc="ve2")
        kvinit_done = p.n("ve2")
        for dc in range(DC):
            p.dma("gpsimd", wB[:, dc * D:(dc + 1) * D], woT_h[dc * 128:(dc + 1) * 128, :],
                  waits=[("te", p.n("te"))] if dc == 0 else [], inc="awl2")
        wo_done = p.n("awl2")

        dstB = out_h if STAGE == 2 else x2_s
        semB = "ost" if STAGE == 2 else "x2st"
        for tg in range(NTG):
            g0 = ([("ve2", p.n("ve2"))] if tg >= 1 else
                  [("ve2", kvinit_done), ("nst0", p.n("nst0")), ("nst1", p.n("nst1")),
                   ("qst0", p.n("qst0")), ("qst1", p.n("qst1")), ("dst", p.n("dst"))])
            p.dma("sync", buf3(sBF[:, 0:4096], DC), grp(q_s, tg * TG), waits=g0, inc="bld")
            p.dma("sync", buf3(sIO[:, 0:4096], DC), grp(num_s, tg * TG), inc="bld")
            p.dma("sync", mx[0:1, 0:TG], den_s[0:1, tg * TG:(tg + 1) * TG], inc="bld")
            p.dma("sync", buf3(sIO[:, 4096:8192], DC), grp(x1_s, tg * TG), inc="bld")
            blb = p.n("bld")
            def f_dmm(t):
                r = None
                for dc in range(DC):
                    r = t.matmul(out=ps1[:], lhsT=ksb[:, 8 + dc:9 + dc], rhs=sBF[:, dc * TG:(dc + 1) * TG],
                                 start=(dc == 0), stop=(dc == DC - 1))
                return r
            p.emit("tensor", f_dmm, waits=[("bld", blb), ("ve2", max(0, p.n("ve2") - 1))], inc="te")
            tw = p.n("te")
            p.emit("vector", lambda v: v.tensor_tensor(out=stat[:, 0:TG], in0=mx[0:1, 0:TG],
                                                       in1=ps1[:], op=ALU.add),
                   waits=[("te", tw), ("bld", blb)], inc="ve2")
            p.emit("vector", lambda v: v.tensor_scalar(out=stat[:, 0:TG], in0=stat[:, 0:TG],
                                                       scalar1=ATT_EPS, scalar2=None, op0=ALU.add),
                   waits=[("ve2", p.n("ve2"))], inc="ve2")
            def f_r0b(s):
                s.activation(out=stat[:, TG:2 * TG], in_=stat[:, 0:TG], func=ACT.Ln)
                return s.activation(out=stat[:, TG:2 * TG], in_=stat[:, TG:2 * TG],
                                    func=ACT.Exp, scale=f32(-1.0))
            p.emit("scalar", f_r0b, waits=[("ve2", p.n("ve2"))], inc="ac")
            p.emit("vector", lambda v: v.tensor_tensor(out=mx[0:1, 0:TG], in0=stat[:, 0:TG],
                                                       in1=stat[:, TG:2 * TG], op=ALU.mult),
                   waits=[("ac", p.n("ac"))], inc="ve2")
            p.emit("vector", lambda v: v.tensor_scalar(out=mx[0:1, 0:TG], in0=mx[0:1, 0:TG],
                                                       scalar1=-1.0, scalar2=2.0, op0=ALU.mult, op1=ALU.add),
                   waits=[("ve2", p.n("ve2"))], inc="ve2")
            p.emit("vector", lambda v: v.tensor_tensor(out=stat[:, TG:2 * TG], in0=stat[:, TG:2 * TG],
                                                       in1=mx[0:1, 0:TG], op=ALU.mult),
                   waits=[("ve2", p.n("ve2"))], inc="ve2")
            vw = p.n("ve2")
            def f_rbc(t):
                return t.matmul(out=pb[2][:], lhsT=ones_f[:], rhs=stat[0:1, TG:2 * TG], start=True, stop=True)
            p.emit("tensor", f_rbc, waits=[("ve2", vw)], inc="te")
            for ec in range(DC):
                bank = ec % 2
                def f_nmm(t, ec=ec, bank=bank):
                    r = None
                    for dc in range(DC):
                        r = t.matmul(out=pb[bank][:],
                                     lhsT=wB[:, 8192 + dc * 1024 + ec * 128: 8192 + dc * 1024 + (ec + 1) * 128],
                                     rhs=sBF[:, dc * TG:(dc + 1) * TG], start=(dc == 0), stop=(dc == DC - 1))
                    return r
                p.emit("tensor", f_nmm, waits=[("ve2", max(0, p.n("ve2") - 1))], inc="te")
                tw = p.n("te")
                def f_no(v, ec=ec, bank=bank):
                    v.tensor_tensor(out=sT[:, 0:TG], in0=pb[bank][:], in1=sIO[:, ec * TG:(ec + 1) * TG], op=ALU.add)
                    return v.tensor_tensor(out=sBF[:, 4096 + ec * TG: 4096 + (ec + 1) * TG],
                                           in0=sT[:, 0:TG], in1=pb[2][:], op=ALU.mult)
                p.emit("vector", f_no, waits=[("te", tw)], inc="ve2")
            oa_done = p.n("ve2")
            for fc in range(DC):
                bank = fc % 2
                w = [("ve2", max(0, p.n("ve2") - 1))]
                if fc == 0:
                    w = [("ve2", oa_done), ("awl2", wo_done)]
                def f_omm(t, fc=fc, bank=bank):
                    r = None
                    for ec in range(DC):
                        r = t.matmul(out=pb[bank][:], lhsT=wB[:, ec * D + fc * 128: ec * D + (fc + 1) * 128],
                                     rhs=sBF[:, 4096 + ec * TG: 4096 + (ec + 1) * TG],
                                     start=(ec == 0), stop=(ec == DC - 1))
                    return r
                p.emit("tensor", f_omm, waits=w, inc="te")
                tw = p.n("te")
                dsem = f"{semB}{fc % 4}"
                def f_x2(v, fc=fc, bank=bank):
                    return v.scalar_tensor_tensor(out=sT[:, 2048 + (fc % 4) * TG:2048 + (fc % 4 + 1) * TG],
                                                  in0=pb[bank][:], scalar=bcol[:, DC + fc:DC + fc + 1],
                                                  in1=sIO[:, 4096 + fc * TG:4096 + (fc + 1) * TG],
                                                  op0=ALU.add, op1=ALU.add)
                p.emit("vector", f_x2, waits=[("te", tw), (dsem, p.n(dsem))], inc="ve2")
                p.dma("sync", dstB[fc * 128:(fc + 1) * 128, tg * TG:(tg + 1) * TG],
                      sT[:, 2048 + (fc % 4) * TG:2048 + (fc % 4 + 1) * TG],
                      waits=[("ve2", p.n("ve2"))], inc=dsem)

    emit_ffn(x_in, x_in, out_h if STAGE == 1 else x1_s, phase=0,
             dst_sem=("ost" if STAGE == 1 else "x1st"))
    if STAGE >= 2:
        p.emit("sync", lambda e: e.nop(),
               waits=[(f"x1st{i}", p.n(f"x1st{i}")) for i in range(4)], inc="sy")
        emit_attn()
    if STAGE >= 3:
        p.emit("sync", lambda e: e.nop(),
               waits=[(f"x2st{i}", p.n(f"x2st{i}")) for i in range(4)], inc="sy")
        emit_ffn(x2_s, x2_s, out_h, phase=1, dst_sem="ost")

    p.emit("sync", lambda e: e.nop(),
           waits=[(f"ost{i}", p.n(f"ost{i}")) for i in range(4)], inc="sy")

    with nc.Block() as block:
        p.run(block)
    return nc


_CACHE = {}


def kernel(**inputs):
    x = np.asarray(inputs["L0_attn_chunk"], dtype=np.float32)
    Wq = np.asarray(inputs["L0_attn_Wq"], dtype=np.float32)
    Wk = np.asarray(inputs["L0_attn_Wk"], dtype=np.float32)
    Wv = np.asarray(inputs["L0_attn_Wv"], dtype=np.float32)
    Wo = np.asarray(inputs["L0_attn_Wo"], dtype=np.float32)
    bq = np.asarray(inputs["L0_attn_bq"], dtype=np.float32)
    bk = np.asarray(inputs["L0_attn_bk"], dtype=np.float32)
    bv = np.asarray(inputs["L0_attn_bv"], dtype=np.float32)
    bo = np.asarray(inputs["L0_attn_bo"], dtype=np.float32)
    wg = np.ascontiguousarray(np.asarray(inputs["L0_ffn_w_gate"], dtype=np.float32))
    wu = np.ascontiguousarray(np.asarray(inputs["L0_ffn_w_up"], dtype=np.float32))
    wd = np.ascontiguousarray(np.asarray(inputs["L0_ffn_w_down"], dtype=np.float32))

    wqT = np.ascontiguousarray(Wq.T)
    wkT = np.ascontiguousarray(Wk.T)
    wvT = np.ascontiguousarray(Wv.T)
    woT = np.ascontiguousarray(Wo.T)
    bqc = np.ascontiguousarray(bq.reshape(DC, 128).T)
    bkc = np.ascontiguousarray(bk.reshape(DC, 128).T)
    boc = np.ascontiguousarray(bo.reshape(DC, 128).T)
    bvr = np.ascontiguousarray(np.broadcast_to(bv.reshape(1, D), (128, D)))

    if "nc" not in _CACHE:
        _CACHE["nc"] = build()
    nc = _CACHE["nc"]

    in_maps = []
    for c in range(NCORE):
        bi = c // 4
        s0 = (c % 4) * TOK
        xs = np.ascontiguousarray(x[bi, s0:s0 + TOK, :].T)
        agmask = np.zeros((128, 4), np.float32)
        agmask[:, : (c % 4)] = 1.0
        in_maps.append(dict(x_in=xs, wqT=wqT, wkT=wkT, wvT=wvT, woT=woT,
                            bq=bqc, bk=bkc, bvr=bvr, bo=boc,
                            wg=wg, wu=wu, wd=wd, agmask=agmask))
    res = run_bass_kernel_spmd(nc, in_maps, core_ids=list(range(NCORE)))
    outp = np.empty((B, T, D), dtype=np.float32)
    for c in range(NCORE):
        bi = c // 4
        s0 = (c % 4) * TOK
        outp[bi, s0:s0 + TOK, :] = res.results[c]["out"].T
    return outp


# version 13
# speedup vs baseline: 1.0277x; 1.0277x over previous
import numpy as np
import concourse.bass as bass
import concourse.mybir as mybir
from concourse.bass_utils import run_bass_kernel_spmd

F32 = mybir.dt.float32
BF16 = mybir.dt.bfloat16
ALU = mybir.AluOpType
ACT = mybir.ActivationFunctionType
AX = mybir.AxisListType

B, T, D, I = 2, 16384, 1024, 3072
NCORE = 8
TOK = (B * T) // NCORE
TG = 512
NTG = TOK // TG
DC = D // 128
OC = I // 128
CH = 1024
NCH = TOK // CH


def set_tok(tok):
    global TOK, NTG, NCH, T
    TOK = tok
    NTG = TOK // TG
    NCH = TOK // CH
    T = (NCORE * TOK) // B
MAGIC = 12582912.0
QMAX = 32767.0
WQMAX = 7.0
LN_EPS = 1e-5
ATT_EPS = 1e-6
STAGE = 3
DBG = False

f32 = np.float32


class P:
    ENGS = ("sync", "vector", "scalar", "tensor", "gpsimd")

    def __init__(self, nc):
        self.nc = nc
        self.q = {e: [] for e in self.ENGS}
        self.sems = {}

    def sem(self, name):
        if name not in self.sems:
            self.sems[name] = [self.nc.alloc_semaphore(name), 0]
        return self.sems[name]

    def emit(self, eng, fn, waits=(), inc=None, inc_by=1):
        if inc is not None:
            self.sem(inc)[1] += inc_by
        ws = tuple((sn, v) for sn, v in waits if v > 0)
        for sn, _ in ws:
            self.sem(sn)
        self.q[eng].append((ws, fn, inc, inc_by))
        return self.sems[inc][1] if inc is not None else None

    def dma(self, eng, out, in_, waits=(), inc=None, **kw):
        if inc is None:
            inc = "d_" + eng
        return self.emit(eng, lambda e, o=out, i=in_, kw=kw: e.dma_start(out=o, in_=i, **kw),
                         waits=waits, inc=inc, inc_by=16)

    def n(self, name):
        return self.sem(name)[1]

    def run(self, block):
        for eng in self.ENGS:
            ops = self.q[eng]

            def make(eng_ops, engname):
                def body(proxy):
                    for waits, fn, inc, inc_by in eng_ops:
                        for (sn, val) in waits:
                            proxy.wait_ge(self.sems[sn][0], val)
                        ins = fn(proxy)
                        if inc is not None:
                            assert ins is not None, f"{engname}: inc {inc} got None"
                            ins.then_inc(self.sems[inc][0], inc_by)
                return body
            getattr(block, eng)(make(ops, eng))


def build():
    nc = bass.Bass("TRN2", target_bir_lowering=False, detect_race_conditions=False)

    x_in = nc.dram_tensor("x_in", [D, TOK], F32, kind="ExternalInput").ap()
    wqT_h = nc.dram_tensor("wqT", [D, D], F32, kind="ExternalInput").ap()
    wkT_h = nc.dram_tensor("wkT", [D, D], F32, kind="ExternalInput").ap()
    wvT_h = nc.dram_tensor("wvT", [D, D], F32, kind="ExternalInput").ap()
    woT_h = nc.dram_tensor("woT", [D, D], F32, kind="ExternalInput").ap()
    bq_h = nc.dram_tensor("bq", [128, DC], F32, kind="ExternalInput").ap()
    bk_h = nc.dram_tensor("bk", [128, DC], F32, kind="ExternalInput").ap()
    bv_h = nc.dram_tensor("bvr", [128, D], F32, kind="ExternalInput").ap()
    bo_h = nc.dram_tensor("bo", [128, DC], F32, kind="ExternalInput").ap()
    wg_h = nc.dram_tensor("wg", [I, D], F32, kind="ExternalInput").ap()
    wu_h = nc.dram_tensor("wu", [I, D], F32, kind="ExternalInput").ap()
    wd_h = nc.dram_tensor("wd", [D, I], F32, kind="ExternalInput").ap()
    agm_h = nc.dram_tensor("agmask", [128, 4], F32, kind="ExternalInput").ap()
    out_h = nc.dram_tensor("out", [D, TOK], F32, kind="ExternalOutput").ap()

    KD = dict(kind="ExternalOutput") if DBG else {}
    ln_s = nc.dram_tensor("ln_s", [D, TOK], F32, **KD).ap()
    xq_s = nc.dram_tensor("xq_s", [D, TOK], BF16, **KD).ap()
    gate_s = nc.dram_tensor("gate_s", [I, TOK], F32, **KD).ap()
    up_s = nc.dram_tensor("up_s", [I, TOK], F32, **KD).ap()
    o_sc = nc.dram_tensor("o_sc", [I, TOK], F32, **KD).ap()
    x1_s = nc.dram_tensor("x1_s", [D, TOK], F32, **KD).ap()
    x2_s = nc.dram_tensor("x2_s", [D, TOK], F32, **KD).ap()
    q_s = nc.dram_tensor("q_s", [D, TOK], BF16, **KD).ap()
    num_s = nc.dram_tensor("num_s", [D, TOK], F32, **KD).ap()
    den_s = nc.dram_tensor("den_s", [1, TOK], F32, **KD).ap()
    wgT_s = nc.dram_tensor("wgT_s", [128, DC * I], BF16, **KD).ap()
    wuT_s = nc.dram_tensor("wuT_s", [128, DC * I], BF16, **KD).ap()
    wdT_s = nc.dram_tensor("wdT_s", [128, OC * D], BF16, **KD).ap()
    cc_in = [nc.dram_tensor(f"cci{i}", [4, 1], F32) for i in range(10)]
    cc_out = [nc.dram_tensor(f"cco{i}", [4, 1], F32) for i in range(10)]
    AGW = DC * D + 8
    ag_in = nc.dram_tensor("ag_in", [128, AGW], F32)
    ag_out = nc.dram_tensor("ag_out", [4 * 128, AGW], F32)

    A = nc.alloc_sbuf_tensor
    wA = A("wA", [128, 24576], BF16)
    wB = A("wB", [128, 24576], BF16)
    sIO = A("sIO", [128, 8192], F32)
    sT = A("sT", [128, 4096], F32)
    wnat = A("wnat", [128, 2048], F32)
    wnatq = A("wnatq", [128, 2048], BF16)
    ktm = A("ktm", [128, 4096], BF16, is_transpose_dest=True)
    sBF = A("sBF", [128, 12288], BF16)
    masks = A("masks", [128, 2048], BF16)
    xbts = A("xbts", [128, 2 * TG], BF16)
    x2ts = A("x2ts", [128, 2 * TG], BF16)
    ident = A("ident", [128, 128], F32)
    bcol = A("bcol", [128, 2 * DC], F32)
    bkcol = A("bkcol", [128, DC], F32)
    bvrow = A("bvrow", [128, D], BF16)
    ones_f = A("ones_f", [1, 128], F32)
    ones_cb = A("ones_cb", [128, 1], BF16)
    stat = A("stat", [1, 1024], F32)
    mx = A("mx", [128, 576], F32)
    smax = A("smax", [1, 128], F32)
    scol = A("scol", [128, 16], F32)
    agm = A("agm", [128, 4], F32)
    ksb = A("ksb", [128, 16], BF16)

    pb = [nc.alloc_psum_tensor(f"pb{i}", [128, TG], F32) for i in range(4)]
    pk = nc.alloc_psum_tensor("pk", [128, 1024], F32)
    ps1 = nc.alloc_psum_tensor("ps1", [1, TG], F32)
    ps2 = nc.alloc_psum_tensor("ps2", [1, TG], F32)

    p = P(nc)

    ktmF = ktm.ap().bitcast(F32)
    scaleS = ktmF[:, 0:768]
    scaleR = ktmF[:, 768:1536]
    scaleT = ktmF[:, 1536:1920]

    SX, SG, SUP, SG2, SO, KSG, KA, NMS = 0, 2, 4, 6, 8, 10, 11, 12

    def gstep(fn):
        w = [("gp", p.n("gp"))]
        p.emit("gpsimd", fn, waits=w, inc="gp")

    gstep(lambda g: g.memset(ones_f[:], 1.0))
    gstep(lambda g: g.memset(ones_cb[:], 1.0))
    gstep(lambda g: g.memset(ident[:], 0.0))
    gstep(lambda g: g.affine_select(out=ident[:], in_=ident[:], compare_op=ALU.not_equal,
                                    fill=1.0, base=0, pattern=[[-1, 128]], channel_multiplier=1))
    for i in range(4):
        gstep(lambda g, i=i: g.memset(masks[:, i * TG:(i + 1) * TG], 1.0))
        if i > 0:
            gstep(lambda g, i=i: g.memset(masks[:, i * TG: i * TG + i * 128], 0.0))
        gstep(lambda g, i=i: g.affine_select(
            out=masks[:, i * TG + i * 128: i * TG + (i + 1) * 128],
            in_=masks[:, i * TG + i * 128: i * TG + (i + 1) * 128],
            compare_op=ALU.is_ge, fill=0.0, base=0, pattern=[[1, 128]], channel_multiplier=-1))
    GP_CONST = p.n("gp")

    p.emit("vector", lambda v: v.memset(smax[:], 0.0), inc="ve")
    p.dma("sync", bcol[:, 0:DC], bq_h)
    p.dma("sync", bcol[:, DC:2 * DC], bo_h)
    p.dma("sync", bkcol[:], bk_h)
    p.dma("sync", agm[:], agm_h)
    p.dma("gpsimd", bvrow[:], bv_h, inc="awl")


    def grp(h, tg0, ntok=TG):
        return h.rearrange("(a p) t -> p a t", p=128)[:, :, tg0:tg0 + ntok]

    def buf3(b, nchunk, ntok=TG):
        return b[:, 0:nchunk * ntok].rearrange("p (a t) -> p a t", t=ntok)

    def wprepA(w_h, oc, h, hw, u, red_hist, free0):
        wn = wnat[:, (u % 2) * 1024:(u % 2) * 1024 + hw]
        nblk = hw // 32
        pv = red_hist.get(u - 2, free0)
        p.dma("sync", wn, w_h[oc * 128:(oc + 1) * 128, h * hw:(h + 1) * hw],
              waits=[("ve", pv)], inc="wld")
        lw = p.n("wld")
        p.emit("vector", lambda v, wn=wn, u=u, nblk=nblk: v.tensor_reduce(
            out=scaleR[:, u * nblk:(u + 1) * nblk].rearrange("p (a b) -> p a b", b=1),
            in_=wn.rearrange("p (a b) -> p a b", b=32),
            axis=AX.X, op=ALU.max, apply_absolute_value=True),
            waits=[("wld", lw)], inc="ve")
        red_hist[u] = p.n("ve")

    def wprep_batch(n):
        p.emit("gpsimd", lambda g: g.tensor_scalar(
            out=scaleS[:, 0:n], in0=scaleR[:, 0:n],
            scalar1=1.0 / WQMAX, scalar2=1e-12, op0=ALU.mult, op1=ALU.max),
            waits=[("ve", p.n("ve")), ("gp", p.n("gp"))], inc="gp")
        def f_r0(s):
            s.activation(out=scaleR[:, 0:n], in_=scaleS[:, 0:n], func=ACT.Ln)
            return s.activation(out=scaleR[:, 0:n], in_=scaleR[:, 0:n],
                                func=ACT.Exp, scale=f32(-1.0))
        p.emit("scalar", f_r0, waits=[("gp", p.n("gp"))], inc="ac")
        def f_nw(v):
            r = None
            for hh in range(0, n, 384):
                w = min(384, n - hh)
                v.tensor_tensor(out=scaleT[:, 0:w], in0=scaleS[:, hh:hh + w],
                                in1=scaleR[:, hh:hh + w], op=ALU.mult)
                v.tensor_scalar(out=scaleT[:, 0:w], in0=scaleT[:, 0:w], scalar1=-1.0,
                                scalar2=2.0, op0=ALU.mult, op1=ALU.add)
                r = v.tensor_tensor(out=scaleR[:, hh:hh + w], in0=scaleR[:, hh:hh + w],
                                    in1=scaleT[:, 0:w], op=ALU.mult)
            return r
        p.emit("vector", f_nw, waits=[("ac", p.n("ac"))], inc="ve")
        return p.n("ve")

    def wprepB(w_h, n_out, oc, h, hw, u, arena, tsem, batch_ve, bq_hist, free0,
               arena_guard=None):
        wn = wnat[:, (u % 2) * 1024:(u % 2) * 1024 + hw]
        wq = wnatq[:, (u % 2) * 1024:(u % 2) * 1024 + hw]
        nblk = hw // 32
        pv = bq_hist.get(u - 2, free0)
        p.dma("sync", wn, w_h[oc * 128:(oc + 1) * 128, h * hw:(h + 1) * hw],
              waits=[("ve", pv)], inc="wld")
        lw = p.n("wld")
        def f_dq(v, wn=wn, wq=wq, u=u, nblk=nblk):
            v.tensor_tensor(
                out=wn.rearrange("p (a b) -> p a b", b=32),
                in0=wn.rearrange("p (a b) -> p a b", b=32),
                in1=scaleR[:, u * nblk:(u + 1) * nblk].rearrange("p (a b) -> p a b", b=1)
                    .broadcast_to([128, nblk, 32]),
                op=ALU.mult)
            v.tensor_scalar(out=wn, in0=wn, scalar1=1.0, scalar2=MAGIC,
                            op0=ALU.mult, op1=ALU.add)
            return v.scalar_tensor_tensor(
                out=wq.rearrange("p (a b) -> p a b", b=32),
                in0=wn.rearrange("p (a b) -> p a b", b=32),
                scalar=MAGIC,
                in1=scaleS[:, u * nblk:(u + 1) * nblk].rearrange("p (a b) -> p a b", b=1)
                    .broadcast_to([128, nblk, 32]),
                op0=ALU.subtract, op1=ALU.mult)
        p.emit("vector", f_dq, waits=[("wld", lw), ("ve", batch_ve)], inc="ve")
        vq = p.n("ve")
        bq_hist[u] = vq
        for kk in range(hw // 128):
            kc = h * (hw // 128) + kk
            w = [("ve", vq)] if kk == 0 else []
            if kk == 0 and arena_guard is not None and u == 0:
                w.append(("te", arena_guard))
            p.dma("sync", arena[:, kc * n_out + oc * 128: kc * n_out + (oc + 1) * 128],
                  wq[:, kk * 128:(kk + 1) * 128], waits=w, inc=tsem, transpose=True)

    def wprep_store(arena, store_h, n_out, n_in, tsem):
        wdone = p.n(tsem)
        for kc in range(n_in // 128):
            p.dma("gpsimd", store_h[:, kc * n_out:(kc + 1) * n_out],
                  arena[:, kc * n_out:(kc + 1) * n_out],
                  waits=[(tsem, wdone)] if kc == 0 else [], inc="wst")
        return wdone

    def emit_ln(src_h, mxc0, store, bf_out, tg_hook=None, single_buf=False):
        for tg in range(NTG):
            bo_ = 0 if single_buf else (tg % 2) * 4096
            buf = sIO[:, bo_:bo_ + 4096]
            lsem = "lnl0" if single_buf else f"lnl{tg % 2}"
            g = [("ve2", p.n("ve2"))] if tg >= (1 if single_buf else 2) else []
            p.dma("sync", buf3(buf, DC), grp(src_h, tg * TG), waits=g, inc=lsem)
            lbatch = p.n(lsem)
            for dc in range(DC):
                db = (dc % 2) * TG
                p.emit("vector", lambda v, buf=buf, dc=dc, db=db: v.tensor_copy(
                    out=xbts[:, db:db + TG], in_=buf[:, dc * TG:(dc + 1) * TG]),
                    waits=[(lsem, lbatch), ("te", max(0, p.n("te") - 1))], inc="ve")
                p.emit("scalar", lambda s, buf=buf, dc=dc, db=db: s.activation(
                    out=x2ts[:, db:db + TG], in_=buf[:, dc * TG:(dc + 1) * TG], func=ACT.Square),
                    waits=[(lsem, lbatch), ("te", max(0, p.n("te") - 1))], inc="ac")
                vw, aw = p.n("ve"), p.n("ac")
                def f_mm(t, dc=dc, db=db):
                    t.matmul(out=ps1[:], lhsT=ones_cb[:], rhs=xbts[:, db:db + TG],
                             start=(dc == 0), stop=(dc == DC - 1))
                    return t.matmul(out=ps2[:], lhsT=ones_cb[:], rhs=x2ts[:, db:db + TG],
                                    start=(dc == 0), stop=(dc == DC - 1))
                w = [("ve", vw), ("ac", aw)]
                if dc == 0:
                    w.append(("ve2", p.n("ve2")))
                    w.append(("gp", GP_CONST))
                p.emit("tensor", f_mm, waits=w, inc="te")
            mmw = p.n("te")
            def f_stats(v):
                v.tensor_scalar(out=stat[:, 0:TG], in0=ps1[:], scalar1=1.0 / D, scalar2=None, op0=ALU.mult)
                return v.tensor_scalar(out=stat[:, TG:2 * TG], in0=ps2[:], scalar1=1.0 / D, scalar2=None, op0=ALU.mult)
            p.emit("vector", f_stats, waits=[("te", mmw)], inc="ve2")
            m2t = x2ts.ap().bitcast(F32)[0:1, 0:TG]
            p.emit("vector", lambda v: v.tensor_tensor(out=m2t, in0=stat[:, 0:TG],
                                                       in1=stat[:, 0:TG], op=ALU.mult),
                   waits=[("ve2", p.n("ve2"))], inc="ve2")
            p.emit("vector", lambda v: v.tensor_tensor(out=stat[:, TG:2 * TG], in0=stat[:, TG:2 * TG],
                                                       in1=m2t, op=ALU.subtract),
                   waits=[("ve2", p.n("ve2"))], inc="ve2")
            def f_r(s):
                s.activation(out=stat[:, TG:2 * TG], in_=stat[:, TG:2 * TG], func=ACT.Ln, bias=f32(LN_EPS))
                return s.activation(out=stat[:, TG:2 * TG], in_=stat[:, TG:2 * TG],
                                    func=ACT.Exp, scale=f32(-0.5))
            p.emit("scalar", f_r, waits=[("ve2", p.n("ve2"))], inc="ac")
            def f_mr(v):
                return v.tensor_tensor(out=stat[:, 0:TG], in0=stat[:, 0:TG],
                                       in1=stat[:, TG:2 * TG], op=ALU.mult)
            p.emit("vector", f_mr, waits=[("ac", p.n("ac"))], inc="ve2")
            vw = p.n("ve2")
            def f_bc(t):
                t.matmul(out=pb[2][:], lhsT=ones_f[:], rhs=stat[0:1, TG:2 * TG], start=True, stop=True)
                return t.matmul(out=pb[3][:], lhsT=ones_f[:], rhs=stat[0:1, 0:TG], start=True, stop=True)
            p.emit("tensor", f_bc, waits=[("ve2", vw), ("ac", p.n("ac")), ("gp", GP_CONST)], inc="te")
            tw = p.n("te")
            for dc in range(DC):
                def f_ln(v, buf=buf, dc=dc):
                    v.tensor_tensor(out=sT[:, 0:TG], in0=buf[:, dc * TG:(dc + 1) * TG], in1=pb[2][:], op=ALU.mult)
                    return v.tensor_tensor(out=buf[:, dc * TG:(dc + 1) * TG], in0=sT[:, 0:TG],
                                           in1=pb[3][:], op=ALU.subtract)
                p.emit("vector", f_ln, waits=[("te", tw)], inc="ve2")
                if bf_out:
                    lb0 = 4096 + (tg % 2) * 4096
                    p.emit("vector", lambda v, buf=buf, dc=dc, lb0=lb0: v.tensor_copy(
                        out=sBF[:, lb0 + dc * TG: lb0 + (dc + 1) * TG],
                        in_=buf[:, dc * TG:(dc + 1) * TG]), inc="ve2")
            if mxc0 is not None:
                p.emit("vector", lambda v, buf=buf, tg=tg: v.tensor_reduce(
                    out=mx[:, mxc0 + tg:mxc0 + tg + 1], in_=buf[:, 0:4096],
                    axis=AX.X, op=ALU.max, apply_absolute_value=True), inc="ve2")
            vw2 = p.n("ve2")
            if store:
                p.dma("sync", grp(ln_s, tg * TG), buf3(buf, DC), waits=[("ve2", vw2)], inc="lnst")
            if tg_hook:
                tg_hook(tg, vw2)

    cc_i = [0]

    def emit_scale(vals, extra_sc=None):
        idx = cc_i[0]
        cc_i[0] += 1
        cin, cout = cc_in[idx], cc_out[idx]
        nv = len(vals)
        w = [("ve2", p.n("ve2")), ("gp2", p.n("gp2"))]
        for j, (c0, ncols, _si) in enumerate(vals):
            p.emit("vector", lambda v, c0=c0, ncols=ncols, j=j: v.tensor_reduce(
                out=mx[:, 472 + j:473 + j], in_=mx[:, c0:c0 + ncols], axis=AX.X, op=ALU.max),
                waits=w, inc="ve")
            w = []
        vw = p.n("ve")
        def f_tr(t):
            return t.transpose(out=pk[0:nv, 0:128], in_=mx[:, 472:472 + nv], identity=ident[:])
        p.emit("tensor", f_tr, waits=[("ve", vw), ("ac", p.n("ac")), ("gp", GP_CONST)], inc="te")
        p.emit("vector", lambda v: v.memset(mx[0:4, 505:506], 0.0),
               waits=[("te", p.n("te"))], inc="ve")
        p.emit("vector", lambda v: v.tensor_reduce(out=mx[0:nv, 505:506], in_=pk[0:nv, 0:128],
                                                   axis=AX.X, op=ALU.max),
               waits=[("ve", p.n("ve"))], inc="ve")
        p.dma("gpsimd", cin.ap()[0:4, :], mx[0:4, 505:506], waits=[("ve", p.n("ve"))], inc="ccd")
        def f_cc(g, cin=cin, cout=cout):
            return g.collective_compute("AllReduce", ALU.max, replica_groups=[list(range(NCORE))],
                                        ins=[cin.ap().opt()], outs=[cout.ap().opt()])
        p.emit("gpsimd", f_cc, waits=[("ccd", p.n("ccd"))], inc="cc")
        p.dma("gpsimd", mx[0:4, 506:507], cout.ap(), waits=[("cc", p.n("cc"))], inc="ccd")
        def f_s(v):
            return v.tensor_scalar(out=mx[0:nv, 507:508], in0=mx[0:nv, 506:507],
                                   scalar1=1.0 / QMAX, scalar2=1e-12, op0=ALU.mult, op1=ALU.max)
        p.emit("vector", f_s, waits=[("ccd", p.n("ccd"))], inc="ve")
        vw = p.n("ve")
        def f_tr2(t):
            return t.transpose(out=pk[0:1, 0:nv], in_=mx[0:nv, 507:508], identity=ident[0:nv, 0:nv])
        p.emit("tensor", f_tr2, waits=[("ve", vw)], inc="te")
        def f_row(v):
            v.tensor_copy(out=smax[0:1, 80:80 + nv], in_=pk[0:1, 0:nv])
            v.memset(smax[0:1, 80 + nv:96], 0.0)
            return v.tensor_copy(out=stat[0:1, 0:nv], in_=pk[0:1, 0:nv])
        p.emit("vector", f_row, waits=[("te", p.n("te"))], inc="ve")
        def f_r0(s):
            s.activation(out=stat[0:1, 16:16 + nv], in_=stat[0:1, 0:nv], func=ACT.Ln)
            return s.activation(out=stat[0:1, 16:16 + nv], in_=stat[0:1, 16:16 + nv],
                                func=ACT.Exp, scale=f32(-1.0))
        p.emit("scalar", f_r0, waits=[("ve", p.n("ve"))], inc="ac")
        p.emit("vector", lambda v: v.tensor_tensor(out=stat[0:1, 32:32 + nv], in0=stat[0:1, 0:nv],
                                                   in1=stat[0:1, 16:16 + nv], op=ALU.mult),
               waits=[("ac", p.n("ac"))], inc="ve")
        p.emit("vector", lambda v: v.tensor_scalar(out=stat[0:1, 32:32 + nv], in0=stat[0:1, 32:32 + nv],
                                                   scalar1=-1.0, scalar2=2.0, op0=ALU.mult, op1=ALU.add),
               waits=[("ve", p.n("ve"))], inc="ve")
        p.emit("vector", lambda v: v.tensor_tensor(out=stat[0:1, 16:16 + nv], in0=stat[0:1, 16:16 + nv],
                                                   in1=stat[0:1, 32:32 + nv], op=ALU.mult),
               waits=[("ve", p.n("ve"))], inc="ve")
        p.emit("vector", lambda v: v.tensor_copy(out=smax[0:1, 88:88 + nv], in_=stat[0:1, 16:16 + nv]),
               waits=[("ve", p.n("ve"))], inc="ve")
        vw = p.n("ve")
        def f_bc(t):
            return t.matmul(out=pb[3][:, 0:16], lhsT=ones_f[:], rhs=smax[0:1, 80:96], start=True, stop=True)
        p.emit("tensor", f_bc, waits=[("ve", vw)], inc="te")
        def f_ev(v):
            r = None
            for j, (_c, _n, si) in enumerate(vals):
                v.tensor_copy(out=scol[:, si:si + 1], in_=pb[3][:, j:j + 1])
                r = v.tensor_copy(out=scol[:, si + 1:si + 2], in_=pb[3][:, 8 + j:9 + j])
            return r
        p.emit("vector", f_ev, waits=[("te", p.n("te"))], inc="ve")
        if extra_sc is not None:
            p.emit("vector", extra_sc, waits=[("ve", p.n("ve"))], inc="ve")
        return p.n("ve")

    def emit_bigmm(rh_s, n_k, warena, n_out, dst_s, mxc0, first_waits, tg_hook=None,
                   rhs_after=None, lhs_after=None, pb_free=None):
        nkc = n_k // 128
        noc = n_out // 128
        rw = nkc * TG
        te_after = {}
        evac_ac = {}
        if pb_free is None:
            pb_free = p.n("ac")
        for tg in range(NTG):
            rb0 = (tg % 2) * rw
            lsem = f"mml{tg % 2}"
            g = [("te", te_after[tg - 2])] if tg >= 2 else (list(first_waits) + [("te", p.n("te"))])
            if rhs_after is not None:
                g = g + [rhs_after[tg]]
            p.dma("scalar", buf3(sBF[:, rb0:rb0 + rw], nkc), grp(rh_s, tg * TG),
                  waits=g, inc=lsem)
            lbatch = p.n(lsem)
            for oc in range(noc):
                k = tg * noc + oc
                bank = k % 4
                w = [(lsem, lbatch),
                     ("ac", evac_ac[k - 4] if k >= 4 else pb_free)]
                if lhs_after is not None:
                    if tg == 0:
                        w += lhs_after(oc)
                elif tg == 0 and oc == 0:
                    w += list(first_waits)
                def f_mm(t, oc=oc, rb0=rb0, bank=bank):
                    r = None
                    for kc in range(nkc):
                        r = t.matmul(out=pb[bank][:],
                                     lhsT=warena[:, kc * n_out + oc * 128: kc * n_out + (oc + 1) * 128],
                                     rhs=sBF[:, rb0 + kc * TG: rb0 + (kc + 1) * TG],
                                     start=(kc == 0), stop=(kc == nkc - 1))
                    return r
                p.emit("tensor", f_mm, waits=w, inc="te")
                tw = p.n("te")
                ev = k % 4
                ssem = f"mst{ev}"
                def f_ev(s, bank=bank, ev=ev):
                    return s.copy(out=sT[:, 2048 + ev * TG:2048 + (ev + 1) * TG], in_=pb[bank][:])
                p.emit("scalar", f_ev, waits=[("te", tw), (ssem, p.n(ssem))], inc="ac")
                aw = p.n("ac")
                evac_ac[k] = aw
                p.emit("vector", lambda v, ev=ev, c=mxc0 + k % 192: v.tensor_reduce(
                    out=mx[:, c:c + 1], in_=sT[:, 2048 + ev * TG:2048 + (ev + 1) * TG], axis=AX.X,
                    op=ALU.max, apply_absolute_value=True), waits=[("ac", aw)], inc="ve")
                p.dma("gpsimd", dst_s[oc * 128:(oc + 1) * 128, tg * TG:(tg + 1) * TG],
                      sT[:, 2048 + ev * TG:2048 + (ev + 1) * TG], waits=[("ac", aw)], inc=ssem)
            te_after[tg] = p.n("te")
            if tg_hook:
                tg_hook(tg)

    g2_hist = {"ve2": {}, "ac": {}, "os": {}}

    def g2_unit(oc, ci, idx, dpass, mxg2c=None, mxoc=None, first_waits=()):
        W = 512
        par = idx % 4
        gin = sIO[:, par * W: par * W + W]
        tA = sT[:, par * W: par * W + W]
        tB = sIO[:, 6144 + par * W: 6144 + par * W + W]
        if dpass:
            upin = sIO[:, 2048 + par * W: 2048 + par * W + W]
            oout = sIO[:, 4096 + par * W: 4096 + par * W + W]
        lsem = f"gl{par % 2}"
        prev = g2_hist["ve2"].get(idx - 4)
        g = ([("ve2", prev)] if prev is not None else list(first_waits))
        ld1 = p.dma("sync", gin,
                    gate_s[oc * 128:(oc + 1) * 128, ci * W:(ci + 1) * W],
                    waits=g, inc=lsem)
        if dpass:
            p.dma("sync", upin,
                  up_s[oc * 128:(oc + 1) * 128, ci * W:(ci + 1) * W], inc=lsem)
        lbatch = p.n(lsem)
        def f_t1(s):
            return s.activation(out=tA, in_=gin, func=ACT.Identity,
                                bias=f32(MAGIC), scale=scol[:, SG + 1:SG + 2])
        p.emit("scalar", f_t1, waits=[(lsem, lbatch)], inc="ac")
        def f_sig(s):
            return s.activation(out=tB, in_=tA, func=ACT.Sigmoid,
                                bias=scol[:, NMS:NMS + 1], scale=scol[:, SG:SG + 1])
        p.emit("scalar", f_sig, inc="ac")
        acw = p.n("ac")
        t4w = None
        if dpass:
            osem = f"os{par % 2}"
            oprev = g2_hist["os"].get(idx - 4)
            def f_t4(s):
                return s.activation(out=oout, in_=upin,
                                    func=ACT.Identity, bias=f32(MAGIC), scale=scol[:, SUP + 1:SUP + 2])
            p.emit("scalar", f_t4, waits=([(osem, oprev)] if oprev is not None else []), inc="ac")
            t4w = p.n("ac")
        def f_g2(v):
            v.tensor_scalar(out=tB, in0=tB, scalar1=65536.0, scalar2=MAGIC,
                            op0=ALU.mult, op1=ALU.add)
            v.tensor_scalar(out=tB, in0=tB, scalar1=MAGIC, scalar2=65535.0,
                            op0=ALU.subtract, op1=ALU.min)
            v.tensor_scalar(out=gin, in0=tA, scalar1=MAGIC, scalar2=scol[:, KSG:KSG + 1],
                            op0=ALU.subtract, op1=ALU.mult)
            r = v.tensor_tensor(out=tB, in0=gin, in1=tB, op=ALU.mult)
            if not dpass:
                r = v.tensor_reduce(out=mx[:, mxg2c:mxg2c + 1], in_=tB,
                                    axis=AX.X, op=ALU.max, apply_absolute_value=True)
            else:
                v.tensor_scalar(out=tA, in0=tB, scalar1=scol[:, SG2 + 1:SG2 + 2], scalar2=MAGIC,
                                op0=ALU.mult, op1=ALU.add)
                r = v.tensor_scalar(out=tB, in0=tA, scalar1=MAGIC, scalar2=scol[:, KA:KA + 1],
                                    op0=ALU.subtract, op1=ALU.mult)
            return r
        p.emit("vector", f_g2, waits=[("ac", acw)], inc="ve2")
        if not dpass:
            g2_hist["ve2"][idx] = p.n("ve2")
        if dpass:
            def f_o(v):
                return v.scalar_tensor_tensor(out=oout, in0=oout,
                                              scalar=MAGIC, in1=tB, op0=ALU.subtract, op1=ALU.mult)
            p.emit("vector", f_o, waits=[("ac", t4w)], inc="ve2")
            p.emit("vector", lambda v, c=mxoc: v.tensor_reduce(
                out=mx[:, c:c + 1], in_=oout,
                axis=AX.X, op=ALU.max, apply_absolute_value=True), inc="ve2")
            g2_hist["ve2"][idx] = p.n("ve2")
            p.dma("gpsimd", o_sc[oc * 128:(oc + 1) * 128, ci * W:(ci + 1) * W],
                  oout, waits=[("ve2", p.n("ve2"))], inc=osem)
            g2_hist["os"][idx] = p.n(osem)

    def emit_ffn(src_h, res_h, dst_h, phase, dst_sem):
        MXLN, MXG, MXUP, MXG2, MXO = 200, 0, 0, 192, 192
        wtg_base = p.n("wtg")
        wtu_base = p.n("wtu")
        ffn_ac0 = p.n("ac")
        wuB_bve = [0]
        wuB_bq = {}
        if phase == 0:
            redg = {}
            for oc in range(OC):
                wprepA(wg_h, oc, 0, D, oc, redg, 0)
            bveg = wprep_batch(768)
            bqg = {}
            fg = p.n("ve")
            for oc in range(OC):
                wprepB(wg_h, I, oc, 0, D, oc, wA, "wtg", bveg, bqg, fg)
            wg_done = wprep_store(wA, wgT_s, I, D, "wtg")
            redu = {}
            for oc in range(OC):
                wprepA(wu_h, oc, 0, D, oc, redu, bqg[OC - 1])
            wuB_bve[0] = wprep_batch(768)
        emit_ln(src_h, MXLN, store=True, bf_out=False)
        emit_scale([(MXLN, NTG, SX)])
        sx_done = p.n("ve")

        if phase == 0:
            pass
        else:
            wgl = [("te", p.n("te")), ("agd", p.n("agd")), ("wst", p.n("wst"))]
            for kc in range(DC):
                p.dma("sync", wA[:, kc * I:(kc + 1) * I], wgT_s[:, kc * I:(kc + 1) * I],
                      waits=(wgl if kc == 0 else []), inc="wtg")
            wg_done = p.n("wtg")

        lnst_done = p.n("lnst")
        xq_after = {}
        xq_ve2 = {}
        for tg in range(NTG):
            bo_ = (tg % 2) * 4096
            buf = sIO[:, bo_:bo_ + 4096]
            lsem = f"xql{tg % 2}"
            g = [("ve2", xq_ve2[tg - 2])] if tg >= 2 else [("lnst", lnst_done)]
            p.dma("sync", buf3(buf, DC), grp(ln_s, tg * TG), waits=g, inc=lsem)
            lbatch = p.n(lsem)
            for dc in range(DC):
                fw = [(lsem, lbatch)]
                if dc == 0:
                    if tg == 0:
                        fw.append(("ve", sx_done))
                    if tg >= 1:
                        fw.append(("xqst", xq_after[tg - 1]))
                def f_q(v, buf=buf, dc=dc):
                    v.tensor_scalar(out=sT[:, 0:TG], in0=buf[:, dc * TG:(dc + 1) * TG],
                                    scalar1=scol[:, SX + 1:SX + 2], scalar2=MAGIC, op0=ALU.mult, op1=ALU.add)
                    return v.tensor_scalar(out=sBF[:, 8192 + dc * TG:8192 + (dc + 1) * TG],
                                           in0=sT[:, 0:TG], scalar1=MAGIC, scalar2=scol[:, SX:SX + 1],
                                           op0=ALU.subtract, op1=ALU.mult)
                p.emit("vector", f_q, waits=fw, inc="ve2")
            vq = p.n("ve2")
            xq_ve2[tg] = vq
            p.dma("gpsimd", grp(xq_s, tg * TG), buf3(sBF[:, 8192:12288], DC),
                  waits=[("ve2", vq)], inc="xqst")
            xq_after[tg] = p.n("xqst")
        xq_done = p.n("xqst")

        if phase == 0:
            upg = (OC + NTG - 1) // NTG
            wuB_f = [None]
            def b1hook(tg):
                if wuB_f[0] is None:
                    wuB_f[0] = p.n("ve")
                for j in range(upg):
                    oc = tg * upg + j
                    if oc < OC:
                        wprepB(wu_h, I, oc, 0, D, oc, wB, "wtu", wuB_bve[0], wuB_bq,
                               wuB_f[0])
        else:
            wul = [("te", p.n("te"))]
            for kc in range(DC):
                p.dma("sync", wB[:, kc * I:(kc + 1) * I], wuT_s[:, kc * I:(kc + 1) * I],
                      waits=(wul if kc == 0 else []), inc="wtu")
            b1hook = None
        if phase == 0:
            lhs_g = lambda oc: [("wtg", wtg_base + 128 * (oc + 1))]
        else:
            lhs_g = lambda oc: [("wtg", wg_done)]
        emit_bigmm(xq_s, D, wA, I, gate_s, MXG, [], tg_hook=b1hook,
                   rhs_after=[("xqst", xq_after[tg]) for tg in range(NTG)], lhs_after=lhs_g,
                   pb_free=ffn_ac0)
        b1_te = p.n("te")
        ac_after_b1 = p.n("ac")
        if phase == 0:
            redd = {}
            fd = wuB_bq[OC - 1]
            for oc in range(DC):
                for h in range(3):
                    wprepA(wd_h, oc, h, 1024, oc * 3 + h, redd, fd)
            bved = wprep_batch(768)
            bqd = {}
            fdB = p.n("ve")
            for oc in range(DC):
                for h in range(3):
                    wprepB(wd_h, D, oc, h, 1024, oc * 3 + h, wA, "wtd", bved, bqd, fdB,
                           arena_guard=b1_te)
            wd_done = wprep_store(wA, wdT_s, D, I, "wtd")
        if phase == 0:
            wu_done = wprep_store(wB, wuT_s, I, D, "wtu")
        else:
            wu_done = p.n("wtu")
        gate_stored = [p.n(f"mst{i}") for i in range(4)]
        def mk_ksg(v):
            v.tensor_scalar(out=scol[:, KSG:KSG + 1], in0=scol[:, SG:SG + 1],
                            scalar1=1.0 / 65536.0, scalar2=None, op0=ALU.mult)
            return v.tensor_scalar(out=scol[:, NMS:NMS + 1], in0=scol[:, SG:SG + 1],
                                   scalar1=-MAGIC, scalar2=None, op0=ALU.mult)
        emit_scale([(MXG, min(192, NTG * OC), SG)], extra_sc=mk_ksg)
        sg_done = p.n("ve")

        idx = [0]
        chain_first = [("ve", sg_done)] + [(f"mst{i}", gate_stored[i]) for i in range(4)]
        def b2hook(tg):
            for j in range(2 * NCH * OC // NTG):
                k = idx[0]
                g2_unit(k // (2 * NCH), k % (2 * NCH) if False else (k % (2 * NCH)) // 2, k,
                        dpass=False, mxg2c=MXG2 + k % 96,
                        first_waits=chain_first) if False else None
                idx[0] += 1
        idx = [0]
        nci = TOK // 512
        g2_hist["ve2"].clear(); g2_hist["os"].clear()
        def b2hook(tg):
            for j in range(OC * nci // NTG):
                k = idx[0]
                g2_unit(k // nci, k % nci, k, dpass=False, mxg2c=MXG2 + k % 384,
                        first_waits=chain_first)
                idx[0] += 1
        if phase == 0:
            lhs_u = lambda oc: [("wtu", wtu_base + 128 * (oc + 1))]
        else:
            lhs_u = lambda oc: [("wtu", wu_done)]
        emit_bigmm(xq_s, D, wB, I, up_s, MXUP, [], tg_hook=b2hook,
                   rhs_after=[("xqst", xq_after[tg]) for tg in range(NTG)], lhs_after=lhs_u,
                   pb_free=ac_after_b1)
        up_mm_done = p.n("te")
        up_stored = [p.n(f"mst{i}") for i in range(4)]
        def mk_ka(v):
            return v.tensor_tensor(out=scol[:, KA:KA + 1], in0=scol[:, SG2:SG2 + 1],
                                   in1=scol[:, SUP:SUP + 1], op=ALU.mult)
        emit_scale([(MXUP, min(192, NTG * OC), SUP), (MXG2, min(384, OC * (TOK // 512)), SG2)], extra_sc=mk_ka)
        sup_done = p.n("ve")

        if phase == 0:
            pass
        else:
            for kc in range(OC):
                p.dma("sync", wA[:, kc * D:(kc + 1) * D], wdT_s[:, kc * D:(kc + 1) * D],
                      waits=[("te", up_mm_done)] if kc == 0 else [], inc="wtd")
            wd_done = p.n("wtd")

        d_first = [("ve", sup_done)] + [(f"mst{i}", up_stored[i]) for i in range(4)]
        g2_hist["ve2"].clear(); g2_hist["os"].clear()
        ndci = TOK // 512
        for k in range(OC * ndci):
            g2_unit(k // ndci, k % ndci, k, dpass=True, mxoc=MXO + k % 192, first_waits=d_first)
        emit_scale([(MXO, min(192, OC * (TOK // 512)), SO)])
        so_done = p.n("ve")
        o_stored = [p.n(f"os{i}") for i in range(2)]

        e_te = {}
        e_oq = {}
        HT = 256
        for htg in range(NTG * 2):
            tg, half = htg // 2, htg % 2
            t0 = tg * TG + half * HT
            ob = (htg % 2) * 6144
            if htg == 0:
                g = [(f"os{i}", o_stored[i]) for i in range(2)] + [("wtd", wd_done), ("ve", so_done)]
            else:
                g = [("ve2", e_oq[htg - 1])]
            p.dma("sync", buf3(sIO[:, 0:6144], OC, HT), grp(o_sc, t0, HT), waits=g, inc="el0")
            lb = p.n("el0")
            def f_oq(v, ob=ob):
                v.tensor_scalar(out=sIO[:, 6144:8192].bitcast(BF16)[:, 0:4096] if False else sT[:, 0:2048],
                                in0=sIO[:, 0:2048], scalar1=scol[:, SO + 1:SO + 2], scalar2=MAGIC,
                                op0=ALU.mult, op1=ALU.add)
                v.tensor_scalar(out=sBF[:, ob:ob + 2048], in0=sT[:, 0:2048],
                                scalar1=MAGIC, scalar2=scol[:, SO:SO + 1], op0=ALU.subtract, op1=ALU.mult)
                v.tensor_scalar(out=sT[:, 0:2048], in0=sIO[:, 2048:4096],
                                scalar1=scol[:, SO + 1:SO + 2], scalar2=MAGIC, op0=ALU.mult, op1=ALU.add)
                v.tensor_scalar(out=sBF[:, ob + 2048:ob + 4096], in0=sT[:, 0:2048],
                                scalar1=MAGIC, scalar2=scol[:, SO:SO + 1], op0=ALU.subtract, op1=ALU.mult)
                v.tensor_scalar(out=sT[:, 0:2048], in0=sIO[:, 4096:6144],
                                scalar1=scol[:, SO + 1:SO + 2], scalar2=MAGIC, op0=ALU.mult, op1=ALU.add)
                return v.tensor_scalar(out=sBF[:, ob + 4096:ob + 6144], in0=sT[:, 0:2048],
                                       scalar1=MAGIC, scalar2=scol[:, SO:SO + 1], op0=ALU.subtract, op1=ALU.mult)
            w = [("el0", lb)]
            if htg >= 2:
                w.append(("te", e_te[htg - 2]))
            p.emit("vector", f_oq, waits=w, inc="ve2")
            e_oq[htg] = p.n("ve2")
            vq = p.n("ve2")
            p.dma("sync", buf3(sIO[:, 6144:8192], DC, HT), grp(res_h, t0, HT),
                  waits=[("ve2", max(0, p.n("ve2") - 1))], inc="rl")
            rlb = p.n("rl")
            for dc in range(DC):
                bank = dc % 4
                w = [("rl", rlb), ("ve2", vq if dc < 4 else max(0, p.n("ve2") - 3))]
                def f_mm(t, dc=dc, bank=bank, ob=ob):
                    r = None
                    for ic in range(OC):
                        r = t.matmul(out=pb[bank][:, 0:HT], lhsT=wA[:, ic * D + dc * 128: ic * D + (dc + 1) * 128],
                                     rhs=sBF[:, ob + ic * HT: ob + (ic + 1) * HT], start=(ic == 0), stop=(ic == OC - 1))
                    return r
                p.emit("tensor", f_mm, waits=w, inc="te")
                tw = p.n("te")
                dsem = f"{dst_sem}{dc % 4}"
                def f_ep(v, dc=dc, bank=bank):
                    return v.scalar_tensor_tensor(out=sT[:, 2048 + (dc % 4) * TG: 2048 + (dc % 4) * TG + HT],
                                                  in0=pb[bank][:, 0:HT], scalar=0.5,
                                                  in1=sIO[:, 6144 + dc * HT:6144 + (dc + 1) * HT],
                                                  op0=ALU.mult, op1=ALU.add)
                p.emit("vector", f_ep, waits=[("te", tw), (dsem, p.n(dsem))], inc="ve2")
                p.dma("gpsimd", dst_h[dc * 128:(dc + 1) * 128, t0:t0 + HT],
                      sT[:, 2048 + (dc % 4) * TG: 2048 + (dc % 4) * TG + HT],
                      waits=[("ve2", p.n("ve2"))], inc=dsem)
            e_te[htg] = p.n("te")

    def emit_attn():
        kfm = sIO.ap().bitcast(BF16)[:, 12288:16384]
        vtm = sT.ap().bitcast(BF16)[:, 4096:8192]
        atm = sIO.ap().bitcast(BF16)[:, 10240:12288]
        kvf = wA.ap().bitcast(F32)[:, 0:8192]
        kvb = wA[:, 16384:24576]

        for dc in range(DC):
            p.dma("gpsimd", wB[:, dc * D:(dc + 1) * D], wqT_h[dc * 128:(dc + 1) * 128, :],
                  waits=[("te", p.n("te"))] if dc == 0 else [], inc="awl")
            p.dma("gpsimd", wB[:, 8192 + dc * D: 8192 + (dc + 1) * D], wkT_h[dc * 128:(dc + 1) * 128, :], inc="awl")
            p.dma("gpsimd", wB[:, 16384 + dc * D: 16384 + (dc + 1) * D], wvT_h[dc * 128:(dc + 1) * 128, :], inc="awl")
        awl_done = p.n("awl")
        zw = [("te", p.n("te")), ("ve2", p.n("ve2")), ("d_sync", p.n("d_sync")), ("d_gpsimd", p.n("d_gpsimd"))]
        p.emit("gpsimd", lambda g: g.memset(kvf, 0.0), waits=zw, inc="gp")
        p.emit("gpsimd", lambda g: g.memset(kvb[:, :], 0.0), waits=[("gp", p.n("gp"))], inc="gp")
        p.emit("gpsimd", lambda g: g.memset(mx[:, 0:16], 0.0), waits=[("gp", p.n("gp"))], inc="gp")
        p.emit("gpsimd", lambda g: g.memset(ksb[:, :], 0.0), waits=[("gp", p.n("gp"))], inc="gp")
        kv_zero = p.n("gp")

        def stage_a(sc, lnw):
            first = (sc == 0)
            LB = 4096 + (sc % 2) * 4096
            hist = {}
            for pj in range(2 * DC):
                isq = pj < DC
                dco = pj % DC
                bank = pj % 4
                warr = 0 if isq else 8192
                w = []
                pv = hist.get(pj - 4)
                if pv is not None:
                    w = [("ve2", pv)]
                elif first and pj == 0:
                    w = [("ve2", lnw), ("awl", awl_done), ("gp", kv_zero)]
                else:
                    w = [("ve2", max(0, p.n("ve2") - 2))]
                def f_pmm(t, dco=dco, bank=bank, warr=warr):
                    r = None
                    for dc in range(DC):
                        r = t.matmul(out=pb[bank][:], lhsT=wB[:, warr + dc * D + dco * 128: warr + dc * D + (dco + 1) * 128],
                                     rhs=sBF[:, LB + dc * TG: LB + (dc + 1) * TG],
                                     start=(dc == 0), stop=(dc == DC - 1))
                    return r
                p.emit("tensor", f_pmm, waits=w, inc="te")
                tw = p.n("te")
                bc = bcol if isq else bkcol
                bi = dco if isq else dco
                tsl = sT[:, (pj % 2) * 2 * TG:(pj % 2) * 2 * TG + TG]
                tsl2 = sT[:, (pj % 2) * 2 * TG + TG:(pj % 2) * 2 * TG + 2 * TG]
                def f_ab(v, bank=bank, bc=bc, bi=bi, tsl=tsl, tsl2=tsl2):
                    v.tensor_scalar(out=tsl, in0=pb[bank][:], scalar1=bc[:, bi:bi + 1],
                                    scalar2=0.0, op0=ALU.add, op1=ALU.max)
                    return v.tensor_scalar(out=tsl2, in0=pb[bank][:], scalar1=bc[:, bi:bi + 1],
                                           scalar2=0.0, op0=ALU.add, op1=ALU.min)
                p.emit("vector", f_ab, waits=[("te", tw)], inc="ve2")
                p.emit("scalar", lambda s, tsl2=tsl2: s.activation(out=tsl2, in_=tsl2, func=ACT.Exp),
                       waits=[("ve2", p.n("ve2"))], inc="ac")
                dst = sBF[:, dco * TG:(dco + 1) * TG] if isq else kfm[:, dco * TG:(dco + 1) * TG]
                def f_po(v, dst=dst, tsl=tsl, tsl2=tsl2):
                    return v.tensor_tensor(out=dst, in0=tsl, in1=tsl2, op=ALU.add)
                qw = [("ac", p.n("ac"))]
                if isq:
                    qw.append((f"qst{(sc + 1) % 2}", p.n(f"qst{(sc + 1) % 2}")))
                p.emit("vector", f_po, waits=qw, inc="ve2")
                hist[pj] = p.n("ve2")
                if isq:
                    p.dma("gpsimd", q_s[dco * 128:(dco + 1) * 128, sc * TG:(sc + 1) * TG],
                          sBF[:, dco * TG:(dco + 1) * TG], waits=[("ve2", p.n("ve2"))], inc=f"qst{sc % 2}")
            k_done = p.n("ve2")
            for tc in range(4):
                for dc in range(DC):
                    w = []
                    if tc == 0 and dc == 0:
                        w = [("ve2", k_done), ("te", p.n("te"))]
                    p.dma("sync", ktm[:, tc * 1024 + dc * 128: tc * 1024 + (dc + 1) * 128],
                          kfm[:, dc * TG + tc * 128: dc * TG + (tc + 1) * 128],
                          waits=w, inc="ktr", transpose=True)
            ktr_done = p.n("ktr")
            for tc in range(4):
                def f_vmm(t, tc=tc):
                    r = None
                    for dc in range(DC):
                        for hh in range(2):
                            r = t.matmul(out=pk[:, hh * 512:(hh + 1) * 512],
                                         lhsT=sBF[:, LB + dc * TG + tc * 128: LB + dc * TG + (tc + 1) * 128],
                                         rhs=wB[:, 16384 + dc * D + hh * 512: 16384 + dc * D + (hh + 1) * 512],
                                         start=(dc == 0), stop=(dc == DC - 1))
                    return r
                p.emit("tensor", f_vmm, waits=[("ve2", p.n("ve2"))], inc="te")
                tw = p.n("te")
                def f_vo(v, tc=tc):
                    return v.tensor_tensor(out=vtm[:, tc * 1024:(tc + 1) * 1024], in0=pk[:],
                                           in1=bvrow[:, :], op=ALU.add)
                p.emit("vector", f_vo, waits=[("te", tw)], inc="ve2")
            vtm_done = p.n("ve2")
            mhist = {}
            for i in range(4):
                ncq = TG - i * 128
                bank = i % 4
                def f_at(t, i=i, bank=bank, ncq=ncq):
                    r = None
                    for dc in range(DC):
                        r = t.matmul(out=pb[bank][:, 0:ncq],
                                     lhsT=kfm[:, dc * TG + i * 128: dc * TG + (i + 1) * 128],
                                     rhs=sBF[:, dc * TG + i * 128: dc * TG + i * 128 + ncq],
                                     start=(dc == 0), stop=(dc == DC - 1))
                    return r
                p.emit("tensor", f_at, waits=[("ve2", vtm_done if i == 0 else p.n("ve2"))], inc="te")
                tw = p.n("te")
                def f_mask(v, i=i, bank=bank, ncq=ncq):
                    return v.tensor_tensor(out=atm[:, i * TG + i * 128: (i + 1) * TG],
                                           in0=pb[bank][:, 0:ncq],
                                           in1=masks[:, i * TG + i * 128:(i + 1) * TG], op=ALU.mult)
                p.emit("vector", f_mask, waits=[("te", tw), ("gp", GP_CONST)], inc="ve2")
            at_done = p.n("ve2")
            def f_den(t):
                t.matmul(out=ps1[:], lhsT=ones_cb[:], rhs=atm[:, 0:TG], start=True, stop=False)
                for i in range(1, 4):
                    t.matmul(out=ps1[0:1, i * 128:TG], lhsT=ones_cb[:],
                             rhs=atm[:, i * TG + i * 128:(i + 1) * TG], start=False, stop=False)
                r = None
                for dc in range(DC):
                    r = t.matmul(out=ps1[:], lhsT=ksb[:, dc:dc + 1], rhs=sBF[:, dc * TG:(dc + 1) * TG],
                                 start=False, stop=(dc == DC - 1))
                return r
            p.emit("tensor", f_den, waits=[("ve2", at_done)], inc="te")
            tw = p.n("te")
            p.emit("vector", lambda v: v.tensor_copy(out=stat[:, 0:TG], in_=ps1[:]),
                   waits=[("te", tw)], inc="ve2")
            p.dma("sync", den_s[0:1, sc * TG:(sc + 1) * TG], stat[0:1, 0:TG],
                  waits=[("ve2", p.n("ve2"))], inc="dst")
            ehist = {}
            for ec in range(DC):
                bank = ec % 4
                nsem = f"nst{ec % 2}"
                w = [("ktr", ktr_done), ("ac", max(0, p.n("ac") - 3))]
                if ec == 0:
                    w.append(("ve2", at_done))
                def f_num(t, ec=ec, bank=bank):
                    t.matmul(out=pb[bank][:], lhsT=vtm[:, 0 * 1024 + ec * 128: 0 * 1024 + (ec + 1) * 128],
                             rhs=atm[:, 0:TG], start=True, stop=False)
                    for i in range(1, 4):
                        t.matmul(out=pb[bank][0:128, i * 128:TG],
                                 lhsT=vtm[:, i * 1024 + ec * 128: i * 1024 + (ec + 1) * 128],
                                 rhs=atm[:, i * TG + i * 128:(i + 1) * TG], start=False, stop=False)
                    r = None
                    for dc in range(DC):
                        r = t.matmul(out=pb[bank][:], lhsT=kvb[:, dc * 1024 + ec * 128: dc * 1024 + (ec + 1) * 128],
                                     rhs=sBF[:, dc * TG:(dc + 1) * TG], start=False, stop=(dc == DC - 1))
                    return r
                p.emit("tensor", f_num, waits=w, inc="te")
                tw = p.n("te")
                ev = ec % 2
                def f_ev(s, bank=bank, ev=ev):
                    return s.copy(out=sT[:, 1024 + ev * TG:1024 + (ev + 1) * TG], in_=pb[bank][:])
                p.emit("scalar", f_ev, waits=[("te", tw), (nsem, p.n(nsem))], inc="ac")
                p.dma("gpsimd", num_s[ec * 128:(ec + 1) * 128, sc * TG:(sc + 1) * TG],
                      sT[:, 1024 + ev * TG:1024 + (ev + 1) * TG], waits=[("ac", p.n("ac"))], inc=nsem)
            for dc in range(DC):
                usepk = (dc % 2 == 0)
                w = [("ktr", ktr_done)]
                pv = ehist.get(dc - 2)
                w.append(("ve2", pv if pv is not None else p.n("ve2")))
                def f_fold(t, dc=dc, usepk=usepk):
                    r = None
                    for tc in range(4):
                        for hh in range(2):
                            out = pk[:, hh * 512:(hh + 1) * 512] if usepk else pb[2 + hh][:]
                            r = t.matmul(out=out,
                                         lhsT=ktm[:, tc * 1024 + dc * 128: tc * 1024 + (dc + 1) * 128],
                                         rhs=vtm[:, tc * 1024 + hh * 512: tc * 1024 + (hh + 1) * 512],
                                         start=(tc == 0), stop=(tc == 3))
                    return r
                p.emit("tensor", f_fold, waits=w, inc="te")
                tw = p.n("te")
                def f_acc(v, dc=dc, usepk=usepk):
                    if usepk:
                        v.tensor_tensor(out=kvf[:, dc * 1024:(dc + 1) * 1024], in0=kvf[:, dc * 1024:(dc + 1) * 1024],
                                        in1=pk[:], op=ALU.add)
                    else:
                        v.tensor_tensor(out=kvf[:, dc * 1024:dc * 1024 + 512], in0=kvf[:, dc * 1024:dc * 1024 + 512],
                                        in1=pb[2][:], op=ALU.add)
                        v.tensor_tensor(out=kvf[:, dc * 1024 + 512:(dc + 1) * 1024], in0=kvf[:, dc * 1024 + 512:(dc + 1) * 1024],
                                        in1=pb[3][:], op=ALU.add)
                    return v.tensor_copy(out=kvb[:, dc * 1024:(dc + 1) * 1024],
                                         in_=kvf[:, dc * 1024:(dc + 1) * 1024])
                p.emit("vector", f_acc, waits=[("te", tw)], inc="ve2")
                ehist[dc] = p.n("ve2")
            def f_ks(t):
                r = None
                for dc in range(DC):
                    for tc in range(4):
                        r = t.matmul(out=pb[0][:, dc:dc + 1],
                                     lhsT=ktm[:, tc * 1024 + dc * 128: tc * 1024 + (dc + 1) * 128],
                                     rhs=ones_cb[:], start=(tc == 0), stop=(tc == 3))
                return r
            p.emit("tensor", f_ks, waits=[("ve2", max(0, p.n("ve2") - DC))], inc="te")
            tw = p.n("te")
            p.emit("vector", lambda v: v.tensor_tensor(out=mx[:, 0:DC], in0=mx[:, 0:DC],
                                                       in1=pb[0][:, 0:DC], op=ALU.add),
                   waits=[("te", tw)], inc="ve2")
            p.emit("vector", lambda v: v.tensor_copy(out=ksb[:, 0:DC], in_=mx[:, 0:DC]),
                   waits=[("ve2", p.n("ve2"))], inc="ve2")

        emit_ln(x1_s, None, store=False, bf_out=True, tg_hook=stage_a, single_buf=True)

        kvw = p.n("ve2")
        for dc in range(DC):
            p.dma("gpsimd", ag_in.ap()[:, dc * D:(dc + 1) * D], kvf[:, dc * 1024:(dc + 1) * 1024],
                  waits=[("ve2", kvw)] if dc == 0 else [], inc="agd")
        p.dma("gpsimd", ag_in.ap()[:, DC * D:DC * D + 8], mx[:, 0:8], inc="agd")
        def f_ag(g):
            return g.collective_compute("AllGather", ALU.bypass,
                                        replica_groups=[[0, 1, 2, 3], [4, 5, 6, 7]],
                                        ins=[ag_in.ap().opt()], outs=[ag_out.ap().opt()])
        p.emit("gpsimd", f_ag, waits=[("agd", p.n("agd"))], inc="cc")
        ag_done = p.n("cc")

        p.emit("gpsimd", lambda g: g.memset(kvf, 0.0),
               waits=[("cc", ag_done), ("d_sync", p.n("d_sync"))], inc="gp")
        p.emit("gpsimd", lambda g: g.memset(mx[:, 8:16], 0.0), waits=[("gp", p.n("gp"))], inc="gp")
        gpz = p.n("gp")
        for j in range(4):
            for dc in range(DC):
                ld = p.dma("sync", sIO[:, 0:1024], ag_out.ap()[j * 128:(j + 1) * 128, dc * D:(dc + 1) * D],
                           waits=[("ve2", p.n("ve2"))] + ([("gp", gpz)] if (j == 0 and dc == 0) else []), inc="agl")
                def f_acc(v, j=j, dc=dc):
                    return v.scalar_tensor_tensor(out=kvf[:, dc * 1024:(dc + 1) * 1024],
                                                  in0=sIO[:, 0:1024], scalar=agm[:, j:j + 1],
                                                  in1=kvf[:, dc * 1024:(dc + 1) * 1024],
                                                  op0=ALU.mult, op1=ALU.add)
                p.emit("vector", f_acc, waits=[("agl", ld)], inc="ve2")
            ld = p.dma("sync", sIO[:, 0:DC], ag_out.ap()[j * 128:(j + 1) * 128, DC * D:DC * D + DC],
                       waits=[("ve2", p.n("ve2"))], inc="agl")
            def f_ksi(v, j=j):
                return v.scalar_tensor_tensor(out=mx[:, 8:8 + DC], in0=sIO[:, 0:DC],
                                              scalar=agm[:, j:j + 1], in1=mx[:, 8:8 + DC],
                                              op0=ALU.mult, op1=ALU.add)
            p.emit("vector", f_ksi, waits=[("agl", ld), ("ve2", p.n("ve2"))], inc="ve2")
        for dc in range(DC):
            p.emit("vector", lambda v, dc=dc: v.tensor_copy(
                out=wB[:, 8192 + dc * 1024: 8192 + (dc + 1) * 1024],
                in_=kvf[:, dc * 1024:(dc + 1) * 1024]),
                waits=[("awl", p.n("awl"))] if dc == 0 else [], inc="ve2")
        p.emit("vector", lambda v: v.tensor_copy(out=ksb[:, 8:16], in_=mx[:, 8:16]),
               waits=[("ve2", p.n("ve2"))], inc="ve2")
        kvinit_done = p.n("ve2")
        for dc in range(DC):
            p.dma("gpsimd", wB[:, dc * D:(dc + 1) * D], woT_h[dc * 128:(dc + 1) * 128, :],
                  waits=[("te", p.n("te"))] if dc == 0 else [], inc="awl2")
        wo_done = p.n("awl2")

        dstB = out_h if STAGE == 2 else x2_s
        semB = "ost" if STAGE == 2 else "x2st"
        for tg in range(NTG):
            g0 = ([("ve2", p.n("ve2"))] if tg >= 1 else
                  [("ve2", kvinit_done), ("nst0", p.n("nst0")), ("nst1", p.n("nst1")),
                   ("qst0", p.n("qst0")), ("qst1", p.n("qst1")), ("dst", p.n("dst"))])
            p.dma("sync", buf3(sBF[:, 0:4096], DC), grp(q_s, tg * TG), waits=g0, inc="bld")
            p.dma("sync", buf3(sIO[:, 0:4096], DC), grp(num_s, tg * TG), inc="bld")
            p.dma("sync", mx[0:1, 0:TG], den_s[0:1, tg * TG:(tg + 1) * TG], inc="bld")
            p.dma("sync", buf3(sIO[:, 4096:8192], DC), grp(x1_s, tg * TG), inc="bld")
            blb = p.n("bld")
            def f_dmm(t):
                r = None
                for dc in range(DC):
                    r = t.matmul(out=ps1[:], lhsT=ksb[:, 8 + dc:9 + dc], rhs=sBF[:, dc * TG:(dc + 1) * TG],
                                 start=(dc == 0), stop=(dc == DC - 1))
                return r
            p.emit("tensor", f_dmm, waits=[("bld", blb), ("ve2", max(0, p.n("ve2") - 1))], inc="te")
            tw = p.n("te")
            p.emit("vector", lambda v: v.tensor_tensor(out=stat[:, 0:TG], in0=mx[0:1, 0:TG],
                                                       in1=ps1[:], op=ALU.add),
                   waits=[("te", tw), ("bld", blb)], inc="ve2")
            p.emit("vector", lambda v: v.tensor_scalar(out=stat[:, 0:TG], in0=stat[:, 0:TG],
                                                       scalar1=ATT_EPS, scalar2=None, op0=ALU.add),
                   waits=[("ve2", p.n("ve2"))], inc="ve2")
            def f_r0b(s):
                s.activation(out=stat[:, TG:2 * TG], in_=stat[:, 0:TG], func=ACT.Ln)
                return s.activation(out=stat[:, TG:2 * TG], in_=stat[:, TG:2 * TG],
                                    func=ACT.Exp, scale=f32(-1.0))
            p.emit("scalar", f_r0b, waits=[("ve2", p.n("ve2"))], inc="ac")
            p.emit("vector", lambda v: v.tensor_tensor(out=mx[0:1, 0:TG], in0=stat[:, 0:TG],
                                                       in1=stat[:, TG:2 * TG], op=ALU.mult),
                   waits=[("ac", p.n("ac"))], inc="ve2")
            p.emit("vector", lambda v: v.tensor_scalar(out=mx[0:1, 0:TG], in0=mx[0:1, 0:TG],
                                                       scalar1=-1.0, scalar2=2.0, op0=ALU.mult, op1=ALU.add),
                   waits=[("ve2", p.n("ve2"))], inc="ve2")
            p.emit("vector", lambda v: v.tensor_tensor(out=stat[:, TG:2 * TG], in0=stat[:, TG:2 * TG],
                                                       in1=mx[0:1, 0:TG], op=ALU.mult),
                   waits=[("ve2", p.n("ve2"))], inc="ve2")
            vw = p.n("ve2")
            def f_rbc(t):
                return t.matmul(out=pb[2][:], lhsT=ones_f[:], rhs=stat[0:1, TG:2 * TG], start=True, stop=True)
            p.emit("tensor", f_rbc, waits=[("ve2", vw)], inc="te")
            for ec in range(DC):
                bank = ec % 2
                def f_nmm(t, ec=ec, bank=bank):
                    r = None
                    for dc in range(DC):
                        r = t.matmul(out=pb[bank][:],
                                     lhsT=wB[:, 8192 + dc * 1024 + ec * 128: 8192 + dc * 1024 + (ec + 1) * 128],
                                     rhs=sBF[:, dc * TG:(dc + 1) * TG], start=(dc == 0), stop=(dc == DC - 1))
                    return r
                p.emit("tensor", f_nmm, waits=[("ve2", max(0, p.n("ve2") - 1))], inc="te")
                tw = p.n("te")
                def f_no(v, ec=ec, bank=bank):
                    v.tensor_tensor(out=sT[:, 0:TG], in0=pb[bank][:], in1=sIO[:, ec * TG:(ec + 1) * TG], op=ALU.add)
                    return v.tensor_tensor(out=sBF[:, 4096 + ec * TG: 4096 + (ec + 1) * TG],
                                           in0=sT[:, 0:TG], in1=pb[2][:], op=ALU.mult)
                p.emit("vector", f_no, waits=[("te", tw)], inc="ve2")
            oa_done = p.n("ve2")
            for fc in range(DC):
                bank = fc % 2
                w = [("ve2", max(0, p.n("ve2") - 1))]
                if fc == 0:
                    w = [("ve2", oa_done), ("awl2", wo_done)]
                def f_omm(t, fc=fc, bank=bank):
                    r = None
                    for ec in range(DC):
                        r = t.matmul(out=pb[bank][:], lhsT=wB[:, ec * D + fc * 128: ec * D + (fc + 1) * 128],
                                     rhs=sBF[:, 4096 + ec * TG: 4096 + (ec + 1) * TG],
                                     start=(ec == 0), stop=(ec == DC - 1))
                    return r
                p.emit("tensor", f_omm, waits=w, inc="te")
                tw = p.n("te")
                dsem = f"{semB}{fc % 4}"
                def f_x2(v, fc=fc, bank=bank):
                    return v.scalar_tensor_tensor(out=sT[:, 2048 + (fc % 4) * TG:2048 + (fc % 4 + 1) * TG],
                                                  in0=pb[bank][:], scalar=bcol[:, DC + fc:DC + fc + 1],
                                                  in1=sIO[:, 4096 + fc * TG:4096 + (fc + 1) * TG],
                                                  op0=ALU.add, op1=ALU.add)
                p.emit("vector", f_x2, waits=[("te", tw), (dsem, p.n(dsem))], inc="ve2")
                p.dma("sync", dstB[fc * 128:(fc + 1) * 128, tg * TG:(tg + 1) * TG],
                      sT[:, 2048 + (fc % 4) * TG:2048 + (fc % 4 + 1) * TG],
                      waits=[("ve2", p.n("ve2"))], inc=dsem)

    emit_ffn(x_in, x_in, out_h if STAGE == 1 else x1_s, phase=0,
             dst_sem=("ost" if STAGE == 1 else "x1st"))
    if STAGE >= 2:
        p.emit("sync", lambda e: e.nop(),
               waits=[(f"x1st{i}", p.n(f"x1st{i}")) for i in range(4)], inc="sy")
        emit_attn()
    if STAGE >= 3:
        p.emit("sync", lambda e: e.nop(),
               waits=[(f"x2st{i}", p.n(f"x2st{i}")) for i in range(4)], inc="sy")
        emit_ffn(x2_s, x2_s, out_h, phase=1, dst_sem="ost")

    p.emit("sync", lambda e: e.nop(),
           waits=[(f"ost{i}", p.n(f"ost{i}")) for i in range(4)], inc="sy")

    with nc.Block() as block:
        p.run(block)
    return nc


_CACHE = {}


def kernel(**inputs):
    x = np.asarray(inputs["L0_attn_chunk"], dtype=np.float32)
    Wq = np.asarray(inputs["L0_attn_Wq"], dtype=np.float32)
    Wk = np.asarray(inputs["L0_attn_Wk"], dtype=np.float32)
    Wv = np.asarray(inputs["L0_attn_Wv"], dtype=np.float32)
    Wo = np.asarray(inputs["L0_attn_Wo"], dtype=np.float32)
    bq = np.asarray(inputs["L0_attn_bq"], dtype=np.float32)
    bk = np.asarray(inputs["L0_attn_bk"], dtype=np.float32)
    bv = np.asarray(inputs["L0_attn_bv"], dtype=np.float32)
    bo = np.asarray(inputs["L0_attn_bo"], dtype=np.float32)
    wg = np.ascontiguousarray(np.asarray(inputs["L0_ffn_w_gate"], dtype=np.float32))
    wu = np.ascontiguousarray(np.asarray(inputs["L0_ffn_w_up"], dtype=np.float32))
    wd = np.ascontiguousarray(np.asarray(inputs["L0_ffn_w_down"], dtype=np.float32))

    wqT = np.ascontiguousarray(Wq.T)
    wkT = np.ascontiguousarray(Wk.T)
    wvT = np.ascontiguousarray(Wv.T)
    woT = np.ascontiguousarray(Wo.T)
    bqc = np.ascontiguousarray(bq.reshape(DC, 128).T)
    bkc = np.ascontiguousarray(bk.reshape(DC, 128).T)
    boc = np.ascontiguousarray(bo.reshape(DC, 128).T)
    bvr = np.ascontiguousarray(np.broadcast_to(bv.reshape(1, D), (128, D)))

    if "nc" not in _CACHE:
        _CACHE["nc"] = build()
    nc = _CACHE["nc"]

    in_maps = []
    for c in range(NCORE):
        bi = c // 4
        s0 = (c % 4) * TOK
        xs = np.ascontiguousarray(x[bi, s0:s0 + TOK, :].T)
        agmask = np.zeros((128, 4), np.float32)
        agmask[:, : (c % 4)] = 1.0
        in_maps.append(dict(x_in=xs, wqT=wqT, wkT=wkT, wvT=wvT, woT=woT,
                            bq=bqc, bk=bkc, bvr=bvr, bo=boc,
                            wg=wg, wu=wu, wd=wd, agmask=agmask))
    res = run_bass_kernel_spmd(nc, in_maps, core_ids=list(range(NCORE)))
    outp = np.empty((B, T, D), dtype=np.float32)
    for c in range(NCORE):
        bi = c // 4
        s0 = (c % 4) * TOK
        outp[bi, s0:s0 + TOK, :] = res.results[c]["out"].T
    return outp
